# version 09
# baseline (speedup 1.0000x reference)
import numpy as np
import ml_dtypes
from contextlib import ExitStack
import concourse.bass as bass
import concourse.mybir as mybir
from concourse.bass_utils import run_bass_kernel_spmd
from concourse.expressions import RuntimeValue

F32 = mybir.dt.float32
BF16 = mybir.dt.bfloat16
U32 = mybir.dt.uint32
AF = mybir.ActivationFunctionType
ALU = mybir.AluOpType
AX = mybir.AxisListType
ENG = mybir.EngineType

PI = float(np.pi)
MAGIC = float(1.5 * 2 ** 23)

DM, TT, DFF, NE = 1024, 2048, 4096, 8
NH, HD = 32, 64
CH, TB = 128, 512
NB, NCPB = TT // TB, TB // CH
FT = 512
NFT = TT // FT
DIN = 2048
DPROJ = 4480

ENGS = ("sync", "gpsimd", "vector", "scalar", "tensor")


class Ctr:
    def __init__(self, sem, step=1):
        self.sem, self.n, self.step = sem, 0, step


class Prog:
    def __init__(self):
        self.q = {e: [] for e in ENGS}

    def do(self, eng, fn, ctr=None, k=None):
        if ctr is None:
            self.q[eng].append(lambda e, fn=fn: fn(e))
            return None
        inc = ctr.step if k is None else k
        ctr.n += inc
        self.q[eng].append(lambda e, fn=fn, c=ctr, i=inc: fn(e).then_inc(c.sem, i))
        return ctr.n

    def wait(self, eng, ctr, val=None):
        v = ctr.n if val is None else val
        if v <= 0:
            return v
        self.q[eng].append(lambda e, s=ctr, v=v: e.wait_ge(s.sem, v))
        return v

    def drain(self, eng, ctr=None, k=None):
        if ctr is None:
            self.q[eng].append(lambda e: e.drain())
            return None
        inc = ctr.step if k is None else k
        ctr.n += inc
        self.q[eng].append(lambda e, c=ctr, i=inc: e.drain().then_inc(c.sem, i))
        return ctr.n


_DBG_STAGES = []


def build_nc():
    nc = bass.Bass("TRN2", target_bir_lowering=False)
    dbg_out = {}
    for st_ in _DBG_STAGES:
        if st_.startswith("x"):
            dbg_out[st_] = nc.dram_tensor("dbg_" + st_, [DM, TT], F32,
                                          kind="ExternalOutput").ap()
        elif st_.startswith("eidx"):
            dbg_out[st_] = nc.dram_tensor("dbg_" + st_, [1, 8], U32,
                                          kind="ExternalOutput").ap()
        elif st_.startswith("sb:"):
            nm, rows, cols, dt = st_.split(":")[1:]
            dbg_out[st_] = nc.dram_tensor("dbg_" + nm, [int(rows), int(cols)],
                                          {"f32": F32, "bf16": BF16}[dt],
                                          kind="ExternalOutput").ap()

    dX = nc.dram_tensor("xT", [DM, TT], F32, kind="ExternalInput").ap()
    dWinT = nc.dram_tensor("w_inT", [DM, DPROJ], F32, kind="ExternalInput").ap()
    dWoutT = nc.dram_tensor("w_outT", [DIN, DM], F32, kind="ExternalInput").ap()
    dWr = nc.dram_tensor("Wr", [DM, NE], F32, kind="ExternalInput").ap()
    dbr = nc.dram_tensor("br", [1, NE], F32, kind="ExternalInput").ap()
    dW1 = nc.dram_tensor("W1", [NE, DM, DFF], F32, kind="ExternalInput").ap()
    db1 = nc.dram_tensor("b1", [NE, DFF], F32, kind="ExternalInput").ap()
    dW2 = nc.dram_tensor("W2", [NE, DFF, DM], F32, kind="ExternalInput").ap()
    db2 = nc.dram_tensor("b2", [NE, DM], F32, kind="ExternalInput").ap()
    ddtb = nc.dram_tensor("dtb", [1, NH], F32, kind="ExternalInput").ap()
    dD = nc.dram_tensor("Dv", [1, NH], F32, kind="ExternalInput").ap()
    dY = nc.dram_tensor("yT", [DM, TT], F32, kind="ExternalOutput").ap()
    dW1s = nc.dram_tensor("w1stage", [DM, DFF], BF16, kind="Internal").ap()
    dW2s = nc.dram_tensor("w2stage", [DFF, DM], BF16, kind="Internal").ap()

    ctx = ExitStack()
    sbt = lambda n, s, d: ctx.enter_context(nc.sbuf_tensor(n, s, d))
    pst = lambda n, s, d=F32: ctx.enter_context(nc.psum_tensor(n, s, d))
    mksem = lambda n, step=1: Ctr(ctx.enter_context(nc.semaphore(n)), step)

    xf = sbt("xf", [128, 8 * TT], F32)
    AR1 = sbt("AR1", [128, 32768], BF16)
    ZO, XO, YO, SO = 0, 8192, 16384, 24576
    AR2 = sbt("AR2", [128, 16384], BF16)
    WSL = 0
    xln = sbt("xln", [128, 8 * FT], BF16)
    w2b = sbt("w2b", [128, 2 * 1024], BF16)
    mrs_b = sbt("mrs_b", [128, TT], BF16)
    rs_b = sbt("rs_b", [128, TT], BF16)
    AR2f = AR2[:].bitcast(F32)
    cosf = AR2[:, 0:1024]
    sinf = AR2[:, 1024:2048]
    costk = AR2[:, 2048:3072]
    sintk = AR2[:, 3072:4096]
    bfz = AR2[:, 4096:4608]
    cff = AR2[:, 4608:5120]
    cshz = AR2[:, 5120:5632]
    brot4 = AR2[:, 5632:6144]
    crot4 = AR2[:, 6144:6656]
    btokz = AR2[:, 6656:7168]
    brt = AR2[:, 7168:7296]
    xft = AR2[:, 7296:7808]
    ebuf = AR2[:, 7808:8064]
    lhr = AR2[0:32, 8064:8192]
    llr = AR2[0:32, 8192:8320]
    krf = AR2[0:64, 8320:8576]
    qrf = AR2[0:64, 8576:8832]
    qhr = AR2[0:64, 8832:9088]
    qhs = AR2[:, 9088:9344]
    krt = AR2[:, 9344:9472]
    vti = AR2[:, 9472:9600]
    vdc = AR2[:, 9600:9728]
    xdh = AR2[:, 9728:9856]
    spt = AR2[:, 9856:10112]
    sbfs = AR2[:, 10112:10240]
    sbfr = AR2[0:64, 10240:10368]
    smalls = AR2f[:, 5248:5760]
    dtv = AR2f[:, 5760:5888]
    admt = AR2f[:, 5888:6016]
    betav = AR2f[:, 6016:6144]
    gammav = AR2f[:, 6144:6272]
    tanga = AR2f[:, 6272:6400]
    sp1 = AR2f[:, 6400:6528]
    sp2 = AR2f[:, 6528:6656]
    negltok = AR2f[:, 6656:6688]
    lfr = AR2f[0:32, 6688:6816]
    dth = AR2f[:, 6816:6944]
    psi = AR2f[:, 6944:7072]
    dbuf = AR2f[:, 7072:7328]
    ptot = AR2f[0:64, 7328:7392]
    pim0 = AR2f[0:32, 7392:7456]
    rc0 = AR2f[0:32, 7456:7460]
    lntmp = AR2f[:, 7460:7972]
    fstage = AR2f[:, 7460:7972]
    kint = AR2[:, 15944:16200].bitcast(mybir.dt.int32)
    xsq = AR2[:, 0:1024]
    xbf = AR2[:, 1024:2048]
    sshp = sbt("sshp", [128, NH * HD], F32)
    srot2 = sbt("srot2", [128, 1024], F32)
    stash = sbt("stash", [128, 16], BF16)
    idbf = sbt("idbf", [128, 128], BF16)
    idf = sbt("idf", [128, 128], F32)
    cneg = sbt("cneg", [128, 128], BF16)
    Lf = sbt("Lf", [128, 128], F32)
    nLf = sbt("nLf", [128, 128], F32)
    ones1 = sbt("ones1", [1, 128], F32)
    onebf = sbt("onebf", [128, 128], BF16)
    hpi = sbt("hpi", [128, 1], F32)
    cst = sbt("cst", [128, 2], F32)
    nc.const_aps.register(F32, 1e-5, cst[:, 0:1])
    nc.const_aps.register(F32, 1.0, cst[:, 1:2])
    dtbc = sbt("dtbc", [128, NH], F32)
    Dbc = sbt("Dbc", [128, NH], F32)
    wrsb = sbt("wrsb", [128, 8 * NE], BF16)
    brsb = sbt("brsb", [1, NE], F32)
    rows_ar = sbt("rows_ar", [1, 2048], F32)
    rowb = rows_ar[0:1, 0:2048]
    b1r = rows_ar[0:1, 0:2048]
    b2r = rows_ar[0:1, 0:1024]
    macc = sbt("macc", [128, 8 * NFT], F32)
    mtmp = sbt("mtmp", [128, 8], F32)
    meank = sbt("meank", [128, NE], BF16)
    scor = sbt("scor", [1, NE], F32)
    mx8 = sbt("mx8", [1, 8], F32)
    eidx = sbt("eidx", [1, 8], U32)
    b1c = sbt("b1c", [128, 32], F32)

    pO = [pst(f"pO{i}", [128, FT]) for i in range(7)]
    pTb = pst("pTb", [128, 1024], BF16)
    pT32 = pTb[:].bitcast(F32)
    PB = [pO[i][:] for i in range(7)] + [pT32]

    sT = mksem("sT")
    sV = mksem("sV")
    sA = mksem("sA")
    dS = mksem("dS", 16)
    dG = mksem("dG", 16)
    dR = mksem("dR", 16)
    wse = [mksem("wse0", 16), mksem("wse1", 16)]
    dSW1 = mksem("dSW1", 16)
    dSW2 = mksem("dSW2", 16)
    dW1L = mksem("dW1L", 16)
    dO = mksem("dO", 16)
    sG = mksem("sG")

    P = Prog()
    T = lambda fn, c=None, k=None: P.do("tensor", fn, c, k)
    V = lambda fn, c=None, k=None: P.do("vector", fn, c, k)
    A = lambda fn, c=None, k=None: P.do("scalar", fn, c, k)
    G = lambda fn, c=None, k=None: P.do("gpsimd", fn, c, k)
    S = lambda fn, c=None, k=None: P.do("sync", fn, c, k)

    for k in range(8):
        S(lambda e, k=k: e.dma_start(out=xf[:, k * TT:(k + 1) * TT],
                                     in_=dX[k * 128:(k + 1) * 128, :]), dS)
    S(lambda e: e.dma_start(out=brsb[:], in_=dbr), dS)
    S(lambda e: e.dma_start(out=dtbc[:], in_=ddtb.to_broadcast([128, NH])), dS)
    S(lambda e: e.dma_start(out=Dbc[:], in_=dD.to_broadcast([128, NH])), dS)
    G(lambda e: e.dma_start(out=wrsb[:].rearrange("p (k n) -> p k n", k=8),
                            in_=dWr.rearrange("(k p) n -> p k n", k=8)), dG)
    LD_X, LD_G0 = dS.n, dG.n

    V(lambda e: e.memset(onebf[:], 1.0))
    V(lambda e: e.memset(idf[:], 1.0))
    V(lambda e: e.memset(Lf[:], 1.0))
    V(lambda e: e.memset(cneg[:], 0.0))
    V(lambda e: e.memset(ones1[:], 1.0))
    V(lambda e: e.memset(hpi[:], PI / 2.0))
    V(lambda e: e.memset(cst[:, 0:1], 1e-5))
    V(lambda e: e.memset(cst[:, 1:2], 1.0))
    V(lambda e: e.memset(sshp[:], 0.0))
    V(lambda e: e.memset(srot2[:], 0.0))
    V(lambda e: e.memset(sbfs, 0.0))
    V(lambda e: e.memset(sbfr, 0.0))
    V(lambda e: e.memset(stash[:], 0.0))
    V(lambda e: e.memset(macc[:], 0.0))
    MS_ONES = P.drain("vector", sV)
    P.wait("gpsimd", sV, MS_ONES)
    G(lambda e: e.affine_select(idbf[:], onebf[:], pattern=[[-1, 128]], base=0,
                                channel_multiplier=1, compare_op=ALU.is_equal, fill=0.0))
    G(lambda e: e.affine_select(idf[:], idf[:], pattern=[[-1, 128]], base=0,
                                channel_multiplier=1, compare_op=ALU.is_equal, fill=0.0))
    G(lambda e: e.affine_select(cneg[:], cneg[:], pattern=[[1, 128]], base=0,
                                channel_multiplier=-1, compare_op=ALU.is_ge,
                                fill=-1e30))
    G(lambda e: e.affine_select(Lf[:], Lf[:], pattern=[[1, 128]], base=0,
                                channel_multiplier=-1, compare_op=ALU.is_ge, fill=0.0))
    MS_AFF = P.drain("gpsimd", sG)
    P.wait("vector", sG, MS_AFF)
    V(lambda e: e.tensor_scalar(nLf[:], Lf[:], -1.0, None, ALU.mult))
    MS_CONST = P.drain("vector", sV)

    P.wait("tensor", sV, MS_CONST)
    P.wait("tensor", sG, MS_AFF)
    P.wait("tensor", dS, LD_X)
    P.wait("tensor", dG, LD_G0)
    P.wait("scalar", sV, MS_CONST)
    P.wait("scalar", dS, LD_X)
    P.wait("vector", dS, LD_X)
    P.wait("gpsimd", dS, LD_X)
    P.wait("sync", sV, MS_CONST)

    RW_S1, RW_S2, RW_RS, RW_MRS = 0, FT, 2 * FT, 3 * FT

    def vchain(*fns):
        for i, f in enumerate(fns):
            if i:
                P.drain("vector")
            V(f)

    def ln_stats_and_bcast():
        P.wait("tensor", sV, sV.n)
        P.wait("tensor", sA, sA.n)
        P.wait("scalar", sT, sT.n)
        P.wait("vector", sT, sT.n)
        for c in range(NFT):
            stat_ms = {}
            for k in range(8):
                par = k % 2
                if k >= 2:
                    P.wait("scalar", sT, stat_ms[k - 2])
                    P.wait("vector", sT, stat_ms[k - 2])
                A(lambda e, k=k, c=c, par=par: e.activation(
                    xsq[:, par * FT:(par + 1) * FT],
                    xf[:, k * TT + c * FT:k * TT + (c + 1) * FT], AF.Square), sA)
                V(lambda e, k=k, c=c, par=par: e.tensor_copy(
                    xbf[:, par * FT:(par + 1) * FT],
                    xf[:, k * TT + c * FT:k * TT + (c + 1) * FT]), sV)
                P.wait("tensor", sA, sA.n)
                P.wait("tensor", sV, sV.n)
                T(lambda e, par=par, k=k: e.matmul(
                    pO[6][0:1, :], onebf[:, 0:1], xbf[:, par * FT:(par + 1) * FT],
                    start=(k == 0), stop=(k == 7)))
                stat_ms[k] = T(lambda e, par=par, k=k: e.matmul(
                    pT32[0:1, :], onebf[:, 0:1], xsq[:, par * FT:(par + 1) * FT],
                    start=(k == 0), stop=(k == 7)), sT)
            P.wait("vector", sT, sT.n)
            vchain(
                lambda e: e.tensor_scalar(rowb[0:1, RW_S1:RW_S1 + FT], pO[6][0:1, :],
                                          1.0 / DM, None, ALU.mult),
                lambda e: e.tensor_scalar(rowb[0:1, RW_S2:RW_S2 + FT], pT32[0:1, :],
                                          1.0 / DM, None, ALU.mult),
                lambda e: e.scalar_tensor_tensor(rowb[0:1, RW_RS:RW_RS + FT],
                                                 rowb[0:1, RW_S1:RW_S1 + FT], -1.0,
                                                 rowb[0:1, RW_S1:RW_S1 + FT],
                                                 ALU.mult, ALU.mult),
                lambda e: e.tensor_tensor(rowb[0:1, RW_S2:RW_S2 + FT],
                                          rowb[0:1, RW_S2:RW_S2 + FT],
                                          rowb[0:1, RW_RS:RW_RS + FT], ALU.add),
            )
            SVV = P.drain("vector", sV)
            P.wait("scalar", sV, SVV)
            SQQ = A(lambda e: e.activation(rowb[0:1, RW_RS:RW_RS + FT],
                                           rowb[0:1, RW_S2:RW_S2 + FT],
                                           AF.Sqrt, bias=1e-5), sA)
            P.wait("vector", sA, SQQ)
            vchain(
                lambda e: e.reciprocal(rowb[0:1, RW_RS:RW_RS + FT],
                                       rowb[0:1, RW_RS:RW_RS + FT]),
                lambda e: e.scalar_tensor_tensor(rowb[0:1, RW_MRS:RW_MRS + FT],
                                                 rowb[0:1, RW_S1:RW_S1 + FT], -1.0,
                                                 rowb[0:1, RW_RS:RW_RS + FT],
                                                 ALU.mult, ALU.mult),
            )
            RWD = P.drain("vector", sV)
            P.wait("tensor", sV, RWD)
            T(lambda e: e.matmul(pO[6][:, :], ones1[0:1, :],
                                 rowb[0:1, RW_RS:RW_RS + FT], start=True, stop=True))
            BCT = T(lambda e: e.matmul(pT32[:, :], ones1[0:1, :],
                                       rowb[0:1, RW_MRS:RW_MRS + FT],
                                       start=True, stop=True), sT)
            P.wait("vector", sT, BCT)
            V(lambda e, c=c: e.tensor_copy(rs_b[:, c * FT:(c + 1) * FT], pO[6][:, :]))
            V(lambda e, c=c: e.tensor_copy(mrs_b[:, c * FT:(c + 1) * FT], pT32[:, :]),
              sV)
            P.wait("tensor", sV, sV.n)

    def ln_apply(c, with_mean_acc=False):
        P.wait("vector", sT, sT.n)
        for k in range(8):
            V(lambda e, k=k, c=c: e.tensor_tensor(
                lntmp[:], xf[:, k * TT + c * FT:k * TT + (c + 1) * FT],
                rs_b[:, c * FT:(c + 1) * FT], ALU.mult))
            P.drain("vector")
            if with_mean_acc:
                V(lambda e, k=k, c=c: e.scalar_tensor_tensor(
                    xln[:, k * FT:(k + 1) * FT], mrs_b[:, c * FT:(c + 1) * FT],
                    1.0, lntmp[:], ALU.mult, ALU.add,
                    accum_out=macc[:, (c * 8 + k):(c * 8 + k) + 1]))
            else:
                V(lambda e, k=k, c=c: e.scalar_tensor_tensor(
                    xln[:, k * FT:(k + 1) * FT], mrs_b[:, c * FT:(c + 1) * FT],
                    1.0, lntmp[:], ALU.mult, ALU.add))
            P.drain("vector")
        return P.drain("vector", sV)

    wstream_i = [0]
    wstream_pe = {}

    def stream_wrow(dram_rows, width):
        j = wstream_i[0]
        wstream_i[0] += 1
        par = j % 2
        if j - 2 in wstream_pe:
            P.wait("gpsimd", sT, wstream_pe[j - 2])
        G(lambda e, dram_rows=dram_rows, par=par, width=width: e.dma_start(
            out=w2b[:, par * 1024:par * 1024 + width], in_=dram_rows), wse[par])
        return j, par, wse[par].n


    def wstream_mark(j, pe_ms):
        wstream_pe[j] = pe_ms

    def routing_and_gather(tag):
        for k in range(8):
            V(lambda e, k=k: e.reduce_sum(
                mtmp[:, k:k + 1], macc[:, k:k + 25:8], axis=AX.X))
        P.drain("vector")
        V(lambda e: e.tensor_scalar(meank[:], mtmp[:], 1.0 / TT, None, ALU.mult))
        MK = P.drain("vector", sV)
        P.wait("tensor", sV, MK)
        SCMM = None
        for k in range(8):
            SCMM = T(lambda e, k=k: e.matmul(
                pO[6][0:1, 0:NE], meank[:, k:k + 1], wrsb[:, k * NE:(k + 1) * NE],
                start=(k == 0), stop=(k == 7)), sT if k == 7 else None)
        P.wait("vector", sT, SCMM)
        vchain(
            lambda e: e.tensor_tensor(scor[:], pO[6][0:1, 0:NE], brsb[:], ALU.add),
            lambda e: e.max(out=mx8[:], in_=scor[:]),
            lambda e: e.max_index(out=eidx[:], in_max=mx8[:], in_values=scor[:]),
        )
        ESEL = P.drain("vector", sV)
        P.wait("gpsimd", sV, ESEL)
        eregW = nc.alloc_register(ENG.Pool, f"eW_{tag}")
        evW = RuntimeValue(eregW, min_val=0, max_val=7)
        G(lambda e: e.reg_load(eregW, eidx[0:1, 0:1]))
        G(lambda e: e.dma_start(out=dW1s, in_=dW1[evW]), dSW1)
        G(lambda e: e.dma_start(out=dW2s, in_=dW2[evW]), dSW2)
        P.wait("sync", dSW1, dSW1.n)
        for k in range(8):
            S(lambda e, k=k: e.dma_start(
                out=AR1[:, k * 4096:(k + 1) * 4096],
                in_=dW1s[k * 128:(k + 1) * 128, :]), dW1L)
        W1Q = dW1L.n
        routing_and_gather.w2q = dSW2.n
        P.wait("sync", sV, ESEL)
        eregB = nc.alloc_register(ENG.SP, f"eB_{tag}")
        evB = RuntimeValue(eregB, min_val=0, max_val=7)
        S(lambda e: e.reg_load(eregB, eidx[0:1, 0:1]))
        for half in range(2):
            S(lambda e, half=half: e.dma_start(
                out=b1r, in_=db1[evB][half * 2048:(half + 1) * 2048].unsqueeze(0)),
                dS)
            BQ = dS.n
            P.wait("tensor", dS, BQ)
            BC = None
            for f in range(16):
                BC = T(lambda e, f=f, half=half: e.matmul(
                    pO[6][:, half * 16 + f:half * 16 + f + 1],
                    b1r[0:1, f * 128:(f + 1) * 128],
                    ones1[0:1, 0:1], start=True, stop=True),
                    sT if f == 15 else None)
            P.wait("sync", sT, BC)
        S(lambda e: e.dma_start(out=b2r, in_=db2[evB].unsqueeze(0)), dS)
        BQ2 = dS.n
        P.wait("vector", dS, BQ2)
        P.wait("vector", sT, sT.n)
        V(lambda e: e.tensor_copy(b1c[:], pO[6][:, 0:32]))
        BCV = P.drain("vector", sV)
        P.wait("tensor", sV, BCV)
        P.wait("scalar", sV, BCV)
        return W1Q

    def ffn(tag):
        P.wait("tensor", sV, sV.n)
        W1Q = routing_and_gather(tag)
        evac_sA = None
        for c in range(NFT):
            XLN = ln_apply(c)
            if tag == "f1" and c == 0:
                dbg_tap_sb("xln1", xln[:])
                dbg_tap_sb("lnrs", rs_b[:])
                dbg_tap_sb("lnmrs", mrs_b[:])
            P.wait("tensor", sV, XLN)
            if c == 0:
                P.wait("tensor", dW1L, W1Q)
            else:
                P.wait("tensor", sV, evac_sA)
            relu_ms = {}
            for f in range(32):
                pa = pO[f % 2]
                if f >= 2:
                    P.wait("tensor", sA, relu_ms[f - 2])
                MMF = None
                for k in range(8):
                    MMF = T(lambda e, f=f, k=k, pa=pa: e.matmul(
                        pa[:], AR1[:, k * 4096 + f * 128:k * 4096 + (f + 1) * 128],
                        xln[:, k * FT:(k + 1) * FT],
                        start=(k == 0), stop=(k == 7)), sT if k == 7 else None)
                P.wait("scalar", sT, MMF)
                relu_ms[f] = A(lambda e, f=f, pa=pa: e.activation(
                    AR2[:, f * FT:(f + 1) * FT], pa[:], AF.Relu,
                    bias=b1c[:, f:f + 1]), sA)
            P.wait("tensor", sA, relu_ms[31])
            if tag == "f1" and c == 0:
                P.wait("sync", sA, sA.n)
                dbg_tap_sb("h1", AR2[:])
                dbg_tap_sb("w1sb", AR1[:])
                dbg_tap_sb("b1c", b1c[:])
            if c == 0:
                P.wait("gpsimd", dSW2, routing_and_gather.w2q)
            last_b = None
            for fk in range(32):
                j, par, wq = stream_wrow(dW2s[fk * 128:(fk + 1) * 128, :], 1024)
                P.wait("tensor", wse[par], wq)
                MMB = None
                for dm in range(8):
                    MMB = T(lambda e, fk=fk, dm=dm, par=par: e.matmul(
                        PB[dm][:, 0:FT],
                        w2b[:, par * 1024 + dm * 128:par * 1024 + (dm + 1) * 128],
                        AR2[:, fk * FT:(fk + 1) * FT],
                        start=(fk == 0), stop=False),
                        sT if dm == 7 else None)
                wstream_mark(j, MMB)
                last_b = MMB
            LB = None
            for dm in range(8):
                for q in range(4):
                    LB = T(lambda e, dm=dm, q=q: e.matmul(
                        PB[dm][:, q * 128:(q + 1) * 128],
                        b2r[0:1, dm * 128:(dm + 1) * 128],
                        ones1[0:1, :], start=False, stop=(q == 3)),
                        sT if (dm == 7 and q == 3) else None)
            P.wait("vector", sT, LB)
            for dm in range(8):
                V(lambda e, dm=dm, c=c: e.scalar_tensor_tensor(
                    xf[:, dm * TT + c * FT:dm * TT + (c + 1) * FT],
                    PB[dm][:, 0:FT], 0.5,
                    xf[:, dm * TT + c * FT:dm * TT + (c + 1) * FT],
                    ALU.mult, ALU.add), sV)
            evac_sA = sV.n

    def softplus_inplace(t):
        SPX = P.drain("vector", sV)
        P.wait("scalar", sV, SPX)
        A(lambda e, t=t: e.activation(sp1, t[:], AF.Abs))
        P.drain("scalar")
        A(lambda e: e.activation(sp1, sp1, AF.Exp, scale=-1.0))
        P.drain("scalar")
        SPA = A(lambda e: e.activation(sp1, sp1, AF.Ln, bias=1.0), sA)
        P.wait("vector", sA, SPA)
        V(lambda e, t=t: e.tensor_relu(sp2, t[:]))
        P.drain("vector")
        V(lambda e, t=t: e.tensor_tensor(t[:], sp2, sp1, ALU.add))
        P.drain("vector")

    def evac_mtile(m, bank):
        if m < 16:
            A(lambda e, m=m, bank=bank: e.activation(
                AR1[:, ZO + m * FT:ZO + (m + 1) * FT], bank[:], AF.Silu))
            return
        if m < 32 or m == 32 or m == 34:
            if m < 32:
                V(lambda e, bank=bank: e.tensor_copy(xft, bank[:]))
            elif m == 32:
                V(lambda e, bank=bank: e.tensor_copy(brot4[0:32, :], bank[0:32, :]))
                V(lambda e, bank=bank: e.tensor_copy(bfz[32:64, :], bank[32:64, :]))
                V(lambda e, bank=bank: e.tensor_copy(bfz[64:128, :], bank[64:128, :]))
                V(lambda e, bank=bank: e.tensor_copy(xft, bank[:]))
            else:
                V(lambda e, bank=bank: e.tensor_copy(lntmp[:], bank[:]))
            XE = P.drain("vector", sV)
            P.wait("tensor", sV, XE)
            TE = None
            for sub in range(4):
                if m == 34:
                    TE = T(lambda e, sub=sub: e.transpose(
                        pT32[:, sub * 128:(sub + 1) * 128],
                        fstage[:, sub * 128:(sub + 1) * 128], idf[:]),
                        sT if sub == 3 else None)
                else:
                    TE = T(lambda e, sub=sub: e.transpose(
                        pTb[:, sub * 128:(sub + 1) * 128],
                        xft[:, sub * 128:(sub + 1) * 128], idbf[:]),
                        sT if sub == 3 else None)
            P.wait("vector", sT, TE)
            if m < 32:
                mx = m - 16
                for sub in range(4):
                    V(lambda e, sub=sub, mx=mx: e.tensor_copy(
                        AR1[:, XO + sub * DIN + mx * 128:XO + sub * DIN + (mx + 1) * 128],
                        pTb[:, sub * 128:(sub + 1) * 128]))
            elif m == 32:
                for sub in range(4):
                    V(lambda e, sub=sub: e.tensor_copy(
                        btokz[:, sub * 128 + 32:(sub + 1) * 128],
                        pTb[:, sub * 128 + 32:(sub + 1) * 128]))
                    V(lambda e, sub=sub: e.tensor_copy(
                        brt[:, sub * 32:(sub + 1) * 32],
                        pTb[:, sub * 128:sub * 128 + 32]))
            else:
                V(lambda e: e.tensor_copy(smalls, pT32[:, 0:512]))
            TEV = P.drain("vector", sV)
            P.wait("tensor", sV, TEV)
            return
        V(lambda e, bank=bank: e.tensor_copy(cff[:, :], bank[:]))
        V(lambda e, bank=bank: e.tensor_copy(crot4[0:32, :], bank[0:32, :]))
        V(lambda e, bank=bank: e.tensor_copy(cshz[32:64, :], bank[32:64, :]))
        V(lambda e, bank=bank: e.tensor_copy(cshz[64:128, :], bank[64:128, :]))

    def emit_chunk(b, cc):
        par = cc % 2
        P.wait("sync", sV, sV.n)
        S(lambda e, cc=cc, par=par: e.dma_start(
            out=AR1[1:128, SO + par * DIN:SO + (par + 1) * DIN],
            in_=AR1[0:127, XO + cc * DIN:XO + (cc + 1) * DIN]), dR)
        if cc == 0:
            S(lambda e, par=par: e.dma_start(
                out=AR1[0:1, SO + par * DIN:SO + par * DIN + DIN],
                in_=stash[:]), dR)
        else:
            S(lambda e, cc=cc, par=par: e.dma_start(
                out=AR1[0:1, SO + par * DIN:SO + par * DIN + DIN],
                in_=AR1[127:128, XO + (cc - 1) * DIN:XO + cc * DIN]), dR)
        XSH = dR.n

        for g in range(8):
            P.wait("vector", sA, sA.n)
            V(lambda e, g=g, cc=cc: e.tensor_tensor(
                dth.rearrange("p (a r) -> p a r", a=4),
                dtv[:, cc * 32 + g * 4:cc * 32 + (g + 1) * 4]
                .unsqueeze(2).to_broadcast([128, 4, 32]),
                tanga[:, cc * 32:(cc + 1) * 32]
                .unsqueeze(1).to_broadcast([128, 4, 32]),
                ALU.mult))
            DTH = P.drain("vector", sV)
            P.wait("tensor", sV, DTH)
            PHI = T(lambda e: e.matmul(pO[6][:, 0:128], dth, Lf[:],
                                       start=True, stop=True), sT)
            P.wait("vector", sT, PHI)
            vchain(
                lambda e: e.tensor_scalar(psi, pO[6][:, 0:128], 1.0 / (2 * PI),
                                          None, ALU.mult),
                lambda e: e.tensor_copy(kint, psi),
                lambda e: e.tensor_copy(sp1, kint),
                lambda e: e.scalar_tensor_tensor(psi, sp1, -2 * PI,
                                                 pO[6][:, 0:128], ALU.mult, ALU.add),
                lambda e: e.tensor_scalar(sp1, psi, PI, None, ALU.is_gt),
                lambda e: e.scalar_tensor_tensor(psi, sp1, -2 * PI, psi,
                                                 ALU.mult, ALU.add),
                lambda e: e.tensor_scalar(sp1, psi, -PI, None, ALU.is_lt),
                lambda e: e.scalar_tensor_tensor(psi, sp1, 2 * PI, psi,
                                                 ALU.mult, ALU.add),
                lambda e: e.tensor_scalar(psi, psi, -PI, PI, ALU.max, ALU.min),
            )
            PSID = P.drain("vector", sV)
            P.wait("scalar", sV, PSID)
            A(lambda e, g=g: e.activation(sinf[:, g * 128:(g + 1) * 128], psi,
                                          AF.Sin))
            A(lambda e: e.activation(sp1, psi, AF.Abs))
            P.drain("scalar")
            PSIA = A(lambda e, g=g: e.activation(cosf[:, g * 128:(g + 1) * 128],
                                                 sp1, AF.Sin,
                                                 bias=hpi[:, 0:1], scale=-1.0), sA)
            P.wait("tensor", sA, PSIA)
        for half in range(4):
            src = (sinf, cosf)[half % 2]
            dst = (sintk, costk)[half % 2]
            goff = (half // 2) * 4
            TE = None
            for g4 in range(4):
                TE = T(lambda e, src=src, g4=g4, goff=goff: e.transpose(
                    pTb[:, g4 * 128:(g4 + 1) * 128],
                    src[:, (goff + g4) * 128:(goff + g4 + 1) * 128], idbf[:]),
                    sT if g4 == 3 else None)
            P.wait("vector", sT, TE)
            for g4 in range(4):
                V(lambda e, dst=dst, g4=g4, goff=goff: e.tensor_copy(
                    dst[:, (goff + g4) * 128:(goff + g4 + 1) * 128],
                    pTb[:, g4 * 128:(g4 + 1) * 128]))
            TEV = P.drain("vector", sV)
            P.wait("tensor", sV, TEV)
        LMM = T(lambda e, cc=cc: e.matmul(pO[6][0:32, 0:128],
                                          admt[:, cc * 32:(cc + 1) * 32], nLf[:],
                                          start=True, stop=True), sT)
        P.wait("vector", sT, LMM)
        P.wait("vector", sA, sA.n)
        vchain(
            lambda e: e.tensor_copy(lfr, pO[6][0:32, 0:128]),
            lambda e: e.tensor_copy(lhr, lfr),
            lambda e: e.tensor_copy(sp1[0:32, 0:128], lhr),
            lambda e: e.tensor_tensor(sp2[0:32, 0:128], lfr, sp1[0:32, 0:128],
                                      ALU.subtract),
            lambda e: e.tensor_copy(llr, sp2[0:32, 0:128]),
        )
        LRV = P.drain("vector", sV)
        P.wait("tensor", sV, LRV)
        LTP = T(lambda e: e.transpose(pT32[:, 0:32], lfr, idf[0:32, 0:32]), sT)
        P.wait("vector", sT, LTP)
        V(lambda e: e.tensor_scalar(negltok, pT32[:, 0:32], -1.0, None, ALU.mult))
        NLT = P.drain("vector", sV)
        P.wait("tensor", sV, NLT)
        P.wait("scalar", sV, NLT)
        P.wait("vector", dR, XSH)

        g_ms, ys_ms = {}, {}
        e_ms, d_ms, spt_ms, post_ms, prep_ms = {}, {}, {}, {}, {}
        BLB = [pO[6][:, 0:128], pT32[:, 0:128]]

        for s in range(NH + 1):
            if s < NH:
                hp = s % 2
                if s >= 2:
                    P.wait("tensor", sA, d_ms[s - 2])
                T(lambda e, s=s, hp=hp: e.matmul(
                    BLB[hp], idbf[0:32, s:s + 1].to_broadcast([32, 128]), lhr,
                    start=True, stop=False))
                BL12 = T(lambda e, s=s, hp=hp: e.matmul(
                    BLB[hp], idbf[0:32, s:s + 1].to_broadcast([32, 128]), llr,
                    start=False, stop=True), sT)
                if s >= 2:
                    P.wait("scalar", sV, post_ms[s - 2])
                P.wait("scalar", sT, BL12)
                e_ms[s] = A(lambda e, s=s, hp=hp: e.activation(
                    ebuf[:, hp * 128:(hp + 1) * 128], BLB[hp], AF.Exp), sA)
            if s >= 1:
                h = s - 1
                hp = h % 2
                P.wait("tensor", sV, prep_ms[h])
                T(lambda e, hp=hp, cc=cc: e.matmul(
                    pO[hp][:, 0:128], bfz[:, cc * 128:(cc + 1) * 128],
                    cff[:, cc * 128:(cc + 1) * 128], start=True, stop=False))
                g_ms[h] = T(lambda e, hp=hp: e.matmul(
                    pO[hp][:, 0:128], krf[:, hp * 128:(hp + 1) * 128],
                    qrf[:, hp * 128:(hp + 1) * 128], start=False, stop=True), sT)
            if s < NH:
                hp = s % 2
                P.wait("tensor", sA, e_ms[s])
                BL3 = T(lambda e, hp=hp: e.matmul(
                    BLB[hp], idbf[:], cneg[:], start=False, stop=True,
                    skip_group_check=True), sT)
                if s >= 2:
                    P.wait("scalar", sV, spt_ms[s - 2])
                P.wait("scalar", sT, BL3)
                d_ms[s] = A(lambda e, s=s, hp=hp: e.activation(
                    dbuf[:, hp * 128:(hp + 1) * 128], BLB[hp], AF.Exp,
                    bias=negltok[:, s:s + 1]), sA)
            if s >= 1:
                h = s - 1
                hp = h % 2
                P.wait("vector", sT, g_ms[h])
                P.wait("vector", sA, d_ms[h])
                V(lambda e, hp=hp: e.tensor_tensor(
                    spt[:, hp * 128:(hp + 1) * 128], pO[hp][:, 0:128],
                    dbuf[:, hp * 128:(hp + 1) * 128], ALU.mult))
                spt_ms[h] = P.drain("vector", sV)
                if h >= 2:
                    P.wait("tensor", sV, post_ms[h - 2])
                P.wait("tensor", sV, spt_ms[h])
                T(lambda e, hp=hp: e.matmul(
                    pO[2 + hp][0:64, 0:128], vti[:, hp * 64:(hp + 1) * 64],
                    spt[:, hp * 128:(hp + 1) * 128], start=True, stop=False))
                T(lambda e, hp=hp: e.matmul(
                    pO[2 + hp][0:64, 0:128], sbfs[:, hp * 64:(hp + 1) * 64],
                    qhs[:, hp * 128:(hp + 1) * 128], start=False, stop=False))
                T(lambda e, hp=hp: e.matmul(
                    pO[2 + hp][0:64, 0:128], sbfr[:, hp * 64:(hp + 1) * 64],
                    qhr[:, hp * 128:(hp + 1) * 128], start=False, stop=False))
                T(lambda e, hp=hp: e.matmul(
                    pO[2 + hp][0:64, 0:128], xdh[:, hp * 64:(hp + 1) * 64],
                    idbf[:], start=False, stop=True))
                T(lambda e, hp=hp, cc=cc: e.matmul(
                    pO[4 + hp][:, 0:64], btokz[:, cc * 128:(cc + 1) * 128],
                    vdc[:, hp * 64:(hp + 1) * 64], start=True, stop=True))
                ys_ms[h] = T(lambda e, hp=hp: e.matmul(
                    pO[4 + hp][0:64, 64:128], krt[:, hp * 64:(hp + 1) * 64],
                    vdc[:, hp * 64:(hp + 1) * 64], start=True, stop=True), sT)
            if s < NH:
                hp = s % 2
                hb = (s % 4) * 32
                gg = s // 4
                if s >= 2:
                    P.wait("vector", sT, ys_ms[s - 2])
                V(lambda e, hp=hp, hb=hb, gg=gg, cc=cc: e.tensor_tensor(
                    krf[0:32, hp * 128:(hp + 1) * 128],
                    brot4[hb:hb + 32, cc * 128:(cc + 1) * 128],
                    cosf[hb:hb + 32, gg * 128:(gg + 1) * 128], ALU.mult))
                V(lambda e, hp=hp, hb=hb, gg=gg, cc=cc: e.tensor_tensor(
                    krf[32:64, hp * 128:(hp + 1) * 128],
                    brot4[hb:hb + 32, cc * 128:(cc + 1) * 128],
                    sinf[hb:hb + 32, gg * 128:(gg + 1) * 128], ALU.mult))
                V(lambda e, hp=hp, hb=hb, gg=gg, cc=cc: e.tensor_tensor(
                    qrf[0:32, hp * 128:(hp + 1) * 128],
                    crot4[hb:hb + 32, cc * 128:(cc + 1) * 128],
                    cosf[hb:hb + 32, gg * 128:(gg + 1) * 128], ALU.mult))
                V(lambda e, hp=hp, hb=hb, gg=gg, cc=cc: e.tensor_tensor(
                    qrf[32:64, hp * 128:(hp + 1) * 128],
                    crot4[hb:hb + 32, cc * 128:(cc + 1) * 128],
                    sinf[hb:hb + 32, gg * 128:(gg + 1) * 128], ALU.mult))
                V(lambda e, s=s, hp=hp, cc=cc: e.tensor_tensor(
                    krt[:, hp * 64:hp * 64 + 32], brt[:, cc * 32:(cc + 1) * 32],
                    costk[:, s * 32:(s + 1) * 32], ALU.mult))
                V(lambda e, s=s, hp=hp, cc=cc: e.tensor_tensor(
                    krt[:, hp * 64 + 32:hp * 64 + 64], brt[:, cc * 32:(cc + 1) * 32],
                    sintk[:, s * 32:(s + 1) * 32], ALU.mult))
                V(lambda e, s=s, cc=cc, par=par: e.tensor_scalar(
                    sp2[:, 0:64],
                    AR1[:, SO + par * DIN + s * 64:SO + par * DIN + (s + 1) * 64],
                    betav[:, cc * 32 + s:cc * 32 + s + 1], None, ALU.mult))
                P.drain("vector")
                V(lambda e, s=s, hp=hp, cc=cc: e.scalar_tensor_tensor(
                    vti[:, hp * 64:(hp + 1) * 64],
                    AR1[:, XO + cc * DIN + s * 64:XO + cc * DIN + (s + 1) * 64],
                    gammav[:, cc * 32 + s:cc * 32 + s + 1], sp2[:, 0:64],
                    ALU.mult, ALU.add))
                V(lambda e, s=s, hp=hp, cc=cc: e.tensor_scalar(
                    xdh[:, hp * 64:(hp + 1) * 64],
                    AR1[:, XO + cc * DIN + s * 64:XO + cc * DIN + (s + 1) * 64],
                    Dbc[:, s:s + 1], None, ALU.mult))
                V(lambda e, s=s, hp=hp: e.tensor_copy(
                    sbfs[:, hp * 64:(hp + 1) * 64], sshp[:, s * 64:(s + 1) * 64]))
                V(lambda e, s=s, hp=hp: e.tensor_copy(
                    sbfr[:, hp * 64:(hp + 1) * 64],
                    srot2[(s % 2) * 64:(s % 2) * 64 + 64,
                          (s // 2) * 64:(s // 2) * 64 + 64]))
                P.wait("vector", sA, e_ms[s])
                V(lambda e, hp=hp: e.tensor_tensor(
                    qhr[:, hp * 128:(hp + 1) * 128],
                    qrf[0:64, hp * 128:(hp + 1) * 128],
                    ebuf[0:64, hp * 128:(hp + 1) * 128], ALU.mult))
                V(lambda e, hp=hp, cc=cc: e.tensor_tensor(
                    qhs[:, hp * 128:(hp + 1) * 128],
                    cshz[:, cc * 128:(cc + 1) * 128],
                    ebuf[:, hp * 128:(hp + 1) * 128], ALU.mult))
                P.wait("vector", sA, d_ms[s])
                P.drain("vector")
                V(lambda e, hp=hp: e.tensor_scalar(
                    vdc[:, hp * 64:(hp + 1) * 64], vti[:, hp * 64:(hp + 1) * 64],
                    dbuf[:, hp * 128 + 127:hp * 128 + 128], None, ALU.mult))
                prep_ms[s] = P.drain("vector", sV)
            if s >= 1:
                h = s - 1
                hp = h % 2
                hb = (h % 4) * 32
                gg = h // 4
                P.wait("vector", sT, ys_ms[h])
                V(lambda e, h=h, hp=hp, cc=cc: e.tensor_tensor(
                    AR1[(h % 2) * 64:(h % 2) * 64 + 64,
                        YO + (h // 2) * FT + cc * 128:YO + (h // 2) * FT + (cc + 1) * 128],
                    pO[2 + hp][0:64, 0:128],
                    AR1[(h % 2) * 64:(h % 2) * 64 + 64,
                        ZO + (h // 2) * FT + cc * 128:ZO + (h // 2) * FT + (cc + 1) * 128],
                    ALU.mult))
                V(lambda e, hb=hb, gg=gg: e.tensor_copy(
                    rc0[:, 0:1], cosf[hb:hb + 32, gg * 128 + 127:gg * 128 + 128]))
                V(lambda e, hb=hb, gg=gg: e.tensor_copy(
                    rc0[:, 1:2], sinf[hb:hb + 32, gg * 128 + 127:gg * 128 + 128]))
                P.drain("vector")
                V(lambda e: e.tensor_scalar(rc0[:, 2:3], rc0[:, 1:2], -1.0,
                                            None, ALU.mult))
                V(lambda e, h=h, hp=hp: e.scalar_tensor_tensor(
                    sshp[:, h * 64:(h + 1) * 64], sshp[:, h * 64:(h + 1) * 64],
                    ebuf[:, hp * 128 + 127:hp * 128 + 128],
                    pO[4 + hp][:, 0:64], ALU.mult, ALU.add))
                V(lambda e, h=h, hp=hp: e.scalar_tensor_tensor(
                    ptot, srot2[(h % 2) * 64:(h % 2) * 64 + 64,
                                (h // 2) * 64:(h // 2) * 64 + 64],
                    ebuf[(h % 2) * 64:(h % 2) * 64 + 64,
                         hp * 128 + 127:hp * 128 + 128],
                    pO[4 + hp][0:64, 64:128], ALU.mult, ALU.add))
                P.drain("vector")
                V(lambda e: e.tensor_copy(pim0, ptot[32:64, :]))
                P.drain("vector")
                V(lambda e: e.tensor_scalar(sp2[0:32, 0:64], ptot[0:32, :],
                                            rc0[:, 0:1], None, ALU.mult))
                P.drain("vector")
                V(lambda e, h=h: e.scalar_tensor_tensor(
                    srot2[(h % 2) * 64:(h % 2) * 64 + 32,
                          (h // 2) * 64:(h // 2) * 64 + 64], pim0, rc0[:, 1:2],
                    sp2[0:32, 0:64], ALU.mult, ALU.add))
                P.drain("vector")
                V(lambda e: e.tensor_scalar(sp2[0:32, 0:64], pim0,
                                            rc0[:, 0:1], None, ALU.mult))
                P.drain("vector")
                V(lambda e, h=h: e.scalar_tensor_tensor(
                    srot2[(h % 2) * 64 + 32:(h % 2) * 64 + 64,
                          (h // 2) * 64:(h // 2) * 64 + 64], ptot[0:32, :],
                    rc0[:, 2:3], sp2[0:32, 0:64], ALU.mult, ALU.add))
                post_ms[h] = P.drain("vector", sV)

    def emit_mamba():
        in_groups = [list(range(6 * i, min(6 * i + 6, 35))) for i in range(6)]
        P.wait("vector", sT, sT.n)
        V(lambda e: e.memset(bfz, 0.0))
        V(lambda e: e.memset(cshz, 0.0))
        V(lambda e: e.memset(btokz, 0.0))
        P.drain("vector", sV)
        for b in range(NB):
            XL2 = ln_apply(b)
            P.wait("tensor", sV, XL2)
            P.wait("gpsimd", sT, sT.n)
            P.wait("vector", dR, dR.n)
            for grp in in_groups:
                gw = len(grp) * 128
                for k in range(8):
                    j, par, wq = stream_wrow(
                        dWinT[k * 128:(k + 1) * 128,
                              grp[0] * 128:grp[0] * 128 + gw], gw)
                    P.wait("tensor", wse[par], wq)
                    MM = None
                    for mi in range(len(grp)):
                        MM = T(lambda e, mi=mi, k=k, par=par: e.matmul(
                            pO[mi][:],
                            w2b[:, par * 1024 + mi * 128:par * 1024 + (mi + 1) * 128],
                            xln[:, k * FT:(k + 1) * FT],
                            start=(k == 0), stop=(k == 7)),
                            sT if mi == len(grp) - 1 else None)
                    wstream_mark(j, MM)
                P.wait("vector", sT, sT.n)
                P.wait("scalar", sT, sT.n)
                for mi, m in enumerate(grp):
                    evac_mtile(m, pO[mi])
                EV = P.drain("vector", sV)
                EA = P.drain("scalar", sA)
                P.wait("tensor", sV, EV)
                P.wait("tensor", sA, EA)
            P.wait("sync", sV, sV.n)
            for t in (brot4, crot4):
                for q in range(1, 4):
                    S(lambda e, t=t, q=q: e.dma_start(
                        out=t[q * 32:(q + 1) * 32, :], in_=t[0:32, :]), dR)
            REP = dR.n
            P.wait("vector", dR, REP)
            sm = smalls.rearrange("p (s d) -> p s d", s=4)
            V(lambda e, sm=sm: e.tensor_tensor(
                dtv.rearrange("p (s d) -> p s d", s=4), sm[:, :, 0:32],
                dtbc[:, 0:32].unsqueeze(1).to_broadcast([128, 4, 32]), ALU.add))
            P.drain("vector")
            softplus_inplace(dtv)
            V(lambda e, sm=sm: e.tensor_copy(
                admt.rearrange("p (s d) -> p s d", s=4), sm[:, :, 32:64]))
            P.drain("vector")
            softplus_inplace(admt)
            vchain(
                lambda e: e.tensor_scalar(admt, admt, 1e-4, None, ALU.max),
                lambda e: e.tensor_tensor(admt, admt, dtv, ALU.mult),
            )
            SMV = P.drain("vector", sV)
            P.wait("scalar", sV, SMV)
            A(lambda e: e.activation(sp1, admt, AF.Exp, scale=-1.0))
            A(lambda e, sm=sm: e.activation(
                sp2.rearrange("p (s d) -> p s d", s=4), sm[:, :, 64:96],
                AF.Sigmoid))
            P.drain("scalar")
            SMA = A(lambda e, sm=sm: e.activation(
                tanga.rearrange("p (s d) -> p s d", s=4), sm[:, :, 96:128],
                AF.Tanh), sA)
            P.wait("vector", sA, SMA)
            vchain(
                lambda e: e.tensor_tensor(gammav, sp2, dtv, ALU.mult),
                lambda e: e.tensor_scalar(sp2, sp2, -1.0, 1.0, ALU.mult,
                                          ALU.add),
                lambda e: e.tensor_tensor(betav, sp2, dtv, ALU.mult),
                lambda e: e.tensor_tensor(betav, betav, sp1, ALU.mult),
                lambda e: e.tensor_scalar(tanga, tanga, PI, None, ALU.mult),
            )
            SMALLD = P.drain("vector", sV)
            P.wait("tensor", sV, SMALLD)
            P.wait("scalar", sV, SMALLD)
            for cc in range(NCPB):
                emit_chunk(b, cc)
            P.wait("tensor", sV, sV.n)
            P.wait("gpsimd", sT, sT.n)
            last_o = None
            for fk in range(16):
                j, par, wq = stream_wrow(dWoutT[fk * 128:(fk + 1) * 128, :], 1024)
                P.wait("tensor", wse[par], wq)
                MMO = None
                for dm in range(8):
                    MMO = T(lambda e, fk=fk, dm=dm, par=par: e.matmul(
                        PB[dm][:, 0:FT],
                        w2b[:, par * 1024 + dm * 128:par * 1024 + (dm + 1) * 128],
                        AR1[:, YO + fk * FT:YO + (fk + 1) * FT],
                        start=(fk == 0), stop=(fk == 15)),
                        sT if dm == 7 else None)
                wstream_mark(j, MMO)
                last_o = MMO
            P.wait("vector", sT, last_o)
            for dm in range(8):
                V(lambda e, dm=dm, b=b: e.tensor_add(
                    xf[:, dm * TT + b * TB:dm * TT + (b + 1) * TB],
                    xf[:, dm * TT + b * TB:dm * TT + (b + 1) * TB],
                    PB[dm][:, 0:FT]), sV)
            OPV = sV.n
            P.wait("tensor", sV, OPV)
            if b < NB - 1:
                P.wait("sync", sV, OPV)
                S(lambda e: e.dma_start(
                    out=stash[:],
                    in_=AR1[127:128, XO + 3 * DIN:XO + 4 * DIN]), dR)
                P.wait("vector", dR, dR.n)

    def dbg_tap_x(name):
        if name not in dbg_out:
            return
        P.wait("sync", sV, sV.n)
        for k in range(8):
            S(lambda e, k=k, name=name: e.dma_start(
                out=dbg_out[name][k * 128:(k + 1) * 128, :],
                in_=xf[:, k * TT:(k + 1) * TT]), dO)
        P.wait("sync", dO, dO.n)

    def dbg_tap_sb(nm, ap):
        key = [s_ for s_ in dbg_out if s_.startswith("sb:" + nm + ":")]
        if not key:
            return
        P.wait("sync", sV, sV.n)
        P.wait("sync", sA, sA.n)
        S(lambda e, k=key[0], ap=ap: e.dma_start(out=dbg_out[k], in_=ap), dO)
        P.wait("sync", dO, dO.n)

    def dbg_tap_eidx(name):
        if name not in dbg_out:
            return
        P.wait("sync", sV, sV.n)
        S(lambda e, name=name: e.dma_start(out=dbg_out[name], in_=eidx[:]), dO)
        P.wait("sync", dO, dO.n)

    ln_stats_and_bcast()
    for c in range(NFT):
        ln_apply(c, with_mean_acc=True)
    ffn("f1")
    dbg_tap_eidx("eidx1")
    dbg_tap_x("x1")
    ln_stats_and_bcast()
    emit_mamba()
    dbg_tap_x("x2")
    ln_stats_and_bcast()
    for c in range(NFT):
        ln_apply(c, with_mean_acc=True)
    ffn("f3")

    FIN = sV.n
    P.wait("sync", sV, FIN)
    for k in range(8):
        S(lambda e, k=k: e.dma_start(out=dY[k * 128:(k + 1) * 128, :],
                                     in_=xf[:, k * TT:(k + 1) * TT]), dO)
    P.wait("sync", dO, dO.n)

    with ctx:
        with nc.Block() as block:
            @block.sync
            def _(e):
                for fn in P.q["sync"]:
                    fn(e)

            @block.gpsimd
            def _(e):
                for fn in P.q["gpsimd"]:
                    fn(e)

            @block.vector
            def _(e):
                for fn in P.q["vector"]:
                    fn(e)

            @block.scalar
            def _(e):
                for fn in P.q["scalar"]:
                    fn(e)

            @block.tensor
            def _(e):
                for fn in P.q["tensor"]:
                    fn(e)
    return nc


def kernel(**inputs):
    x = np.ascontiguousarray(inputs["L0_attn_u"])
    in_common = {
        "w_inT": np.ascontiguousarray(inputs["L0_attn_w_in"].T),
        "w_outT": np.ascontiguousarray(inputs["L0_attn_w_out"].T),
        "Wr": np.ascontiguousarray(inputs["L0_ffn_Wr"]),
        "br": np.ascontiguousarray(inputs["L0_ffn_br"]).reshape(1, NE),
        "W1": np.ascontiguousarray(inputs["L0_ffn_W1"]),
        "b1": np.ascontiguousarray(inputs["L0_ffn_b1"]),
        "W2": np.ascontiguousarray(inputs["L0_ffn_W2"]),
        "b2": np.ascontiguousarray(inputs["L0_ffn_b2"]),
        "dtb": np.ascontiguousarray(inputs["L0_attn_dt_bias"]).reshape(1, NH),
        "Dv": np.ascontiguousarray(inputs["L0_attn_D"]).reshape(1, NH),
    }
    in_maps = []
    for b in range(8):
        m = dict(in_common)
        m["xT"] = np.ascontiguousarray(x[b].T)
        in_maps.append(m)
    nc = build_nc()
    res = run_bass_kernel_spmd(nc, in_maps, core_ids=list(range(8)))
    out = np.stack([np.asarray(r["yT"]).T for r in res.results], axis=0)
    return out.astype(np.float32)


# version 11
# speedup vs baseline: 1.0634x; 1.0634x over previous
import numpy as np
import ml_dtypes
from contextlib import ExitStack
import concourse.bass as bass
import concourse.mybir as mybir
from concourse.bass_utils import run_bass_kernel_spmd
from concourse.expressions import RuntimeValue

F32 = mybir.dt.float32
BF16 = mybir.dt.bfloat16
U32 = mybir.dt.uint32
AF = mybir.ActivationFunctionType
ALU = mybir.AluOpType
AX = mybir.AxisListType
ENG = mybir.EngineType

PI = float(np.pi)
MAGIC = float(1.5 * 2 ** 23)

DM, TT, DFF, NE = 1024, 2048, 4096, 8
NH, HD = 32, 64
CH, TB = 128, 512
NB, NCPB = TT // TB, TB // CH
FT = 512
NFT = TT // FT
DIN = 2048
DPROJ = 4480

ENGS = ("sync", "gpsimd", "vector", "scalar", "tensor")


class Ctr:
    def __init__(self, sem, step=1):
        self.sem, self.n, self.step = sem, 0, step


class Prog:
    def __init__(self):
        self.q = {e: [] for e in ENGS}

    def do(self, eng, fn, ctr=None, k=None):
        if ctr is None:
            self.q[eng].append(lambda e, fn=fn: fn(e))
            return None
        inc = ctr.step if k is None else k
        ctr.n += inc
        self.q[eng].append(lambda e, fn=fn, c=ctr, i=inc: fn(e).then_inc(c.sem, i))
        return ctr.n

    def wait(self, eng, ctr, val=None):
        v = ctr.n if val is None else val
        if v <= 0:
            return v
        self.q[eng].append(lambda e, s=ctr, v=v: e.wait_ge(s.sem, v))
        return v

    def drain(self, eng, ctr=None, k=None):
        if ctr is None:
            self.q[eng].append(lambda e: e.drain())
            return None
        inc = ctr.step if k is None else k
        ctr.n += inc
        self.q[eng].append(lambda e, c=ctr, i=inc: e.drain().then_inc(c.sem, i))
        return ctr.n


_DBG_STAGES = []


def build_nc():
    nc = bass.Bass("TRN2", target_bir_lowering=False)
    dbg_out = {}
    for st_ in _DBG_STAGES:
        if st_.startswith("x"):
            dbg_out[st_] = nc.dram_tensor("dbg_" + st_, [DM, TT], F32,
                                          kind="ExternalOutput").ap()
        elif st_.startswith("eidx"):
            dbg_out[st_] = nc.dram_tensor("dbg_" + st_, [1, 8], U32,
                                          kind="ExternalOutput").ap()
        elif st_.startswith("sb:"):
            nm, rows, cols, dt = st_.split(":")[1:]
            dbg_out[st_] = nc.dram_tensor("dbg_" + nm, [int(rows), int(cols)],
                                          {"f32": F32, "bf16": BF16}[dt],
                                          kind="ExternalOutput").ap()

    dX = nc.dram_tensor("xT", [DM, TT], F32, kind="ExternalInput").ap()
    dWinT = nc.dram_tensor("w_inT", [DM, DPROJ], F32, kind="ExternalInput").ap()
    dWoutT = nc.dram_tensor("w_outT", [DIN, DM], F32, kind="ExternalInput").ap()
    dWr = nc.dram_tensor("Wr", [DM, NE], F32, kind="ExternalInput").ap()
    dbr = nc.dram_tensor("br", [1, NE], F32, kind="ExternalInput").ap()
    dW1 = nc.dram_tensor("W1", [NE, DM, DFF], F32, kind="ExternalInput").ap()
    db1 = nc.dram_tensor("b1", [NE, DFF], F32, kind="ExternalInput").ap()
    dW2 = nc.dram_tensor("W2", [NE, DFF, DM], F32, kind="ExternalInput").ap()
    db2 = nc.dram_tensor("b2", [NE, DM], F32, kind="ExternalInput").ap()
    ddtb = nc.dram_tensor("dtb", [1, NH], F32, kind="ExternalInput").ap()
    dD = nc.dram_tensor("Dv", [1, NH], F32, kind="ExternalInput").ap()
    dY = nc.dram_tensor("yT", [DM, TT], F32, kind="ExternalOutput").ap()
    dW1s = nc.dram_tensor("w1stage", [DM, DFF], BF16, kind="Internal").ap()
    dW2s = nc.dram_tensor("w2stage", [DFF, DM], BF16, kind="Internal").ap()

    ctx = ExitStack()
    sbt = lambda n, s, d: ctx.enter_context(nc.sbuf_tensor(n, s, d))
    pst = lambda n, s, d=F32: ctx.enter_context(nc.psum_tensor(n, s, d))
    mksem = lambda n, step=1: Ctr(ctx.enter_context(nc.semaphore(n)), step)

    xf = sbt("xf", [128, 8 * TT], F32)
    AR1 = sbt("AR1", [128, 32768], BF16)
    ZO, XO, YO, SO = 0, 8192, 16384, 24576
    AR2 = sbt("AR2", [128, 16384], BF16)
    WSL = 0
    xln = sbt("xln", [128, 8 * FT], BF16)
    w2b = sbt("w2b", [128, 2 * 1024], BF16)
    mrs_b = sbt("mrs_b", [128, TT], BF16)
    rs_b = sbt("rs_b", [128, TT], BF16)
    AR2f = AR2[:].bitcast(F32)
    cosf = AR2[:, 0:1024]
    sinf = AR2[:, 1024:2048]
    costk = AR2[:, 2048:3072]
    sintk = AR2[:, 3072:4096]
    bfz = AR2[:, 4096:4608]
    cff = AR2[:, 4608:5120]
    cshz = AR2[:, 5120:5632]
    brot4 = AR2[:, 5632:6144]
    crot4 = AR2[:, 6144:6656]
    btokz = AR2[:, 6656:7168]
    brt = AR2[:, 7168:7296]
    xft = AR2[:, 7296:7808]
    ebuf = AR2[:, 7808:8064]
    lhr = AR2[0:32, 8064:8192]
    llr = AR2[0:32, 8192:8320]
    krf = AR2[0:64, 8320:8576]
    qrf = AR2[0:64, 8576:8832]
    qhr = AR2[0:64, 8832:9088]
    qhs = AR2[:, 9088:9344]
    krt = AR2[:, 9344:9472]
    vti = AR2[:, 9472:9600]
    vdc = AR2[:, 9600:9728]
    xdh = AR2[:, 9728:9856]
    spt = AR2[:, 9856:10112]
    smalls = AR2f[:, 5248:5760]
    dtv = AR2f[:, 5760:5888]
    admt = AR2f[:, 5888:6016]
    betav = AR2f[:, 6016:6144]
    gammav = AR2f[:, 6144:6272]
    tanga = AR2f[:, 6272:6400]
    sp1 = AR2f[:, 6400:6528]
    sp2 = AR2f[:, 6528:6656]
    negltok = AR2f[:, 6656:6688]
    lfr = AR2f[0:32, 6688:6816]
    dth = AR2f[:, 6816:6944]
    psi = AR2f[:, 6944:7072]
    dbuf = AR2f[:, 7072:7328]
    ptot = AR2f[0:64, 7328:7392]
    pim0 = AR2f[0:32, 7392:7456]
    rc0 = AR2f[0:32, 7456:7460]
    lntmp = AR2f[:, 7460:7972]
    fstage = AR2f[:, 7460:7972]
    kint = AR2[:, 15944:16200].bitcast(mybir.dt.int32)
    wcall = AR2f[:, 7972:8004]
    vall = AR1[:, 28672:30720]
    vdall = AR1[:, 30720:32768]
    xsq = AR2[:, 0:1024]
    xbf = AR2[:, 1024:2048]
    sshp = sbt("sshp", [128, NH * HD], BF16)
    srot_bf = sbt("srot_bf", [64, NH * HD], BF16)
    stash = sbt("stash", [128, 16], BF16)
    idbf = sbt("idbf", [128, 128], BF16)
    idf = sbt("idf", [128, 128], F32)
    cneg = sbt("cneg", [128, 128], BF16)
    Lf = sbt("Lf", [128, 128], F32)
    nLf = sbt("nLf", [128, 128], F32)
    Lsn = sbt("Lsn", [128, 128], F32)
    ones1 = sbt("ones1", [1, 128], F32)
    onebf = sbt("onebf", [128, 128], BF16)
    hpi = sbt("hpi", [128, 1], F32)
    cst = sbt("cst", [128, 2], F32)
    nc.const_aps.register(F32, 1e-5, cst[:, 0:1])
    nc.const_aps.register(F32, 1.0, cst[:, 1:2])
    dtbc = sbt("dtbc", [128, NH], F32)
    Dbc = sbt("Dbc", [128, NH], F32)
    wrsb = sbt("wrsb", [128, 8 * NE], BF16)
    brsb = sbt("brsb", [1, NE], F32)
    rows_ar = sbt("rows_ar", [1, 2048], F32)
    rowb = rows_ar[0:1, 0:2048]
    b1r = rows_ar[0:1, 0:2048]
    b2r = rows_ar[0:1, 0:1024]
    macc = sbt("macc", [128, 8 * NFT], F32)
    mtmp = sbt("mtmp", [128, 8], F32)
    meank = sbt("meank", [128, NE], BF16)
    scor = sbt("scor", [1, NE], F32)
    mx8 = sbt("mx8", [1, 8], F32)
    eidx = sbt("eidx", [1, 8], U32)
    b1c = sbt("b1c", [128, 32], F32)

    pO = [pst(f"pO{i}", [128, FT]) for i in range(7)]
    pTb = pst("pTb", [128, 1024], BF16)
    pT32 = pTb[:].bitcast(F32)
    PB = [pO[i][:] for i in range(7)] + [pT32]

    sT = mksem("sT")
    sV = mksem("sV")
    sA = mksem("sA")
    dS = mksem("dS", 16)
    dG = mksem("dG", 16)
    dR = mksem("dR", 16)
    wse = [mksem("wse0", 16), mksem("wse1", 16)]
    dSW1 = mksem("dSW1", 16)
    dSW2 = mksem("dSW2", 16)
    dW1L = mksem("dW1L", 16)
    dO = mksem("dO", 16)
    sG = mksem("sG")

    P = Prog()
    T = lambda fn, c=None, k=None: P.do("tensor", fn, c, k)
    V = lambda fn, c=None, k=None: P.do("vector", fn, c, k)
    A = lambda fn, c=None, k=None: P.do("scalar", fn, c, k)
    G = lambda fn, c=None, k=None: P.do("gpsimd", fn, c, k)
    S = lambda fn, c=None, k=None: P.do("sync", fn, c, k)

    for k in range(8):
        S(lambda e, k=k: e.dma_start(out=xf[:, k * TT:(k + 1) * TT],
                                     in_=dX[k * 128:(k + 1) * 128, :]), dS)
    S(lambda e: e.dma_start(out=brsb[:], in_=dbr), dS)
    S(lambda e: e.dma_start(out=dtbc[:], in_=ddtb.to_broadcast([128, NH])), dS)
    S(lambda e: e.dma_start(out=Dbc[:], in_=dD.to_broadcast([128, NH])), dS)
    G(lambda e: e.dma_start(out=wrsb[:].rearrange("p (k n) -> p k n", k=8),
                            in_=dWr.rearrange("(k p) n -> p k n", k=8)), dG)
    LD_X, LD_G0 = dS.n, dG.n

    V(lambda e: e.memset(onebf[:], 1.0))
    V(lambda e: e.memset(idf[:], 1.0))
    V(lambda e: e.memset(Lf[:], 1.0))
    V(lambda e: e.memset(cneg[:], 0.0))
    V(lambda e: e.memset(ones1[:], 1.0))
    V(lambda e: e.memset(hpi[:], PI / 2.0))
    V(lambda e: e.memset(cst[:, 0:1], 1e-5))
    V(lambda e: e.memset(cst[:, 1:2], 1.0))
    V(lambda e: e.memset(sshp[:], 0.0))
    V(lambda e: e.memset(srot_bf[:], 0.0))
    V(lambda e: e.memset(stash[:], 0.0))
    V(lambda e: e.memset(macc[:], 0.0))
    MS_ONES = P.drain("vector", sV)
    P.wait("gpsimd", sV, MS_ONES)
    G(lambda e: e.affine_select(idbf[:], onebf[:], pattern=[[-1, 128]], base=0,
                                channel_multiplier=1, compare_op=ALU.is_equal, fill=0.0))
    G(lambda e: e.affine_select(idf[:], idf[:], pattern=[[-1, 128]], base=0,
                                channel_multiplier=1, compare_op=ALU.is_equal, fill=0.0))
    G(lambda e: e.affine_select(cneg[:], cneg[:], pattern=[[1, 128]], base=0,
                                channel_multiplier=-1, compare_op=ALU.is_ge,
                                fill=-1e30))
    G(lambda e: e.affine_select(Lf[:], Lf[:], pattern=[[1, 128]], base=0,
                                channel_multiplier=-1, compare_op=ALU.is_ge, fill=0.0))
    MS_AFF = P.drain("gpsimd", sG)
    P.wait("vector", sG, MS_AFF)
    V(lambda e: e.tensor_scalar(nLf[:], Lf[:], -1.0, None, ALU.mult))
    V(lambda e: e.tensor_scalar(Lsn[:], Lf[:], -1.0, None, ALU.add))
    MS_CONST = P.drain("vector", sV)

    P.wait("tensor", sV, MS_CONST)
    P.wait("tensor", sG, MS_AFF)
    P.wait("tensor", dS, LD_X)
    P.wait("tensor", dG, LD_G0)
    P.wait("scalar", sV, MS_CONST)
    P.wait("scalar", dS, LD_X)
    P.wait("vector", dS, LD_X)
    P.wait("gpsimd", dS, LD_X)
    P.wait("sync", sV, MS_CONST)

    RW_S1, RW_S2, RW_RS, RW_MRS = 0, FT, 2 * FT, 3 * FT

    def vchain(*fns):
        for i, f in enumerate(fns):
            if i:
                P.drain("vector")
            V(f)

    def ln_stats_and_bcast():
        P.wait("tensor", sV, sV.n)
        P.wait("tensor", sA, sA.n)
        P.wait("scalar", sT, sT.n)
        P.wait("vector", sT, sT.n)
        for c in range(NFT):
            stat_ms = {}
            for k in range(8):
                par = k % 2
                if k >= 2:
                    P.wait("scalar", sT, stat_ms[k - 2])
                    P.wait("vector", sT, stat_ms[k - 2])
                A(lambda e, k=k, c=c, par=par: e.activation(
                    xsq[:, par * FT:(par + 1) * FT],
                    xf[:, k * TT + c * FT:k * TT + (c + 1) * FT], AF.Square), sA)
                V(lambda e, k=k, c=c, par=par: e.tensor_copy(
                    xbf[:, par * FT:(par + 1) * FT],
                    xf[:, k * TT + c * FT:k * TT + (c + 1) * FT]), sV)
                P.wait("tensor", sA, sA.n)
                P.wait("tensor", sV, sV.n)
                T(lambda e, par=par, k=k: e.matmul(
                    pO[6][0:1, :], onebf[:, 0:1], xbf[:, par * FT:(par + 1) * FT],
                    start=(k == 0), stop=(k == 7)))
                stat_ms[k] = T(lambda e, par=par, k=k: e.matmul(
                    pT32[0:1, :], onebf[:, 0:1], xsq[:, par * FT:(par + 1) * FT],
                    start=(k == 0), stop=(k == 7)), sT)
            P.wait("vector", sT, sT.n)
            vchain(
                lambda e: e.tensor_scalar(rowb[0:1, RW_S1:RW_S1 + FT], pO[6][0:1, :],
                                          1.0 / DM, None, ALU.mult),
                lambda e: e.tensor_scalar(rowb[0:1, RW_S2:RW_S2 + FT], pT32[0:1, :],
                                          1.0 / DM, None, ALU.mult),
                lambda e: e.scalar_tensor_tensor(rowb[0:1, RW_RS:RW_RS + FT],
                                                 rowb[0:1, RW_S1:RW_S1 + FT], -1.0,
                                                 rowb[0:1, RW_S1:RW_S1 + FT],
                                                 ALU.mult, ALU.mult),
                lambda e: e.tensor_tensor(rowb[0:1, RW_S2:RW_S2 + FT],
                                          rowb[0:1, RW_S2:RW_S2 + FT],
                                          rowb[0:1, RW_RS:RW_RS + FT], ALU.add),
            )
            SVV = P.drain("vector", sV)
            P.wait("scalar", sV, SVV)
            SQQ = A(lambda e: e.activation(rowb[0:1, RW_RS:RW_RS + FT],
                                           rowb[0:1, RW_S2:RW_S2 + FT],
                                           AF.Sqrt, bias=1e-5), sA)
            P.wait("vector", sA, SQQ)
            vchain(
                lambda e: e.reciprocal(rowb[0:1, RW_RS:RW_RS + FT],
                                       rowb[0:1, RW_RS:RW_RS + FT]),
                lambda e: e.scalar_tensor_tensor(rowb[0:1, RW_MRS:RW_MRS + FT],
                                                 rowb[0:1, RW_S1:RW_S1 + FT], -1.0,
                                                 rowb[0:1, RW_RS:RW_RS + FT],
                                                 ALU.mult, ALU.mult),
            )
            RWD = P.drain("vector", sV)
            P.wait("tensor", sV, RWD)
            T(lambda e: e.matmul(pO[6][:, :], ones1[0:1, :],
                                 rowb[0:1, RW_RS:RW_RS + FT], start=True, stop=True))
            BCT = T(lambda e: e.matmul(pT32[:, :], ones1[0:1, :],
                                       rowb[0:1, RW_MRS:RW_MRS + FT],
                                       start=True, stop=True), sT)
            P.wait("vector", sT, BCT)
            V(lambda e, c=c: e.tensor_copy(rs_b[:, c * FT:(c + 1) * FT], pO[6][:, :]))
            V(lambda e, c=c: e.tensor_copy(mrs_b[:, c * FT:(c + 1) * FT], pT32[:, :]),
              sV)
            P.wait("tensor", sV, sV.n)

    def ln_apply(c, with_mean_acc=False):
        P.wait("vector", sT, sT.n)
        xin3 = xf[:].rearrange("p (k t) -> p k t", k=8)[:, :, c * FT:(c + 1) * FT]
        xln3 = xln[:].rearrange("p (k t) -> p k t", k=8)
        rsb3 = rs_b[:, c * FT:(c + 1) * FT].unsqueeze(1).to_broadcast([128, 8, FT])
        mrs3 = mrs_b[:, c * FT:(c + 1) * FT].unsqueeze(1).to_broadcast([128, 8, FT])
        V(lambda e, c=c: e.tensor_tensor(xln3, xin3, rsb3, ALU.mult))
        P.drain("vector")
        V(lambda e, c=c: e.tensor_tensor(xln3, xln3, mrs3, ALU.add))
        P.drain("vector")
        if with_mean_acc:
            for k in range(8):
                V(lambda e, k=k, c=c: e.reduce_sum(
                    macc[:, (c * 8 + k):(c * 8 + k) + 1],
                    xln[:, k * FT:(k + 1) * FT], axis=AX.X))
        return P.drain("vector", sV)

    wstream_i = [0]
    wstream_pe = {}

    def stream_wrow(dram_rows, width):
        j = wstream_i[0]
        wstream_i[0] += 1
        par = j % 2
        if j - 2 in wstream_pe:
            P.wait("gpsimd", sT, wstream_pe[j - 2])
        G(lambda e, dram_rows=dram_rows, par=par, width=width: e.dma_start(
            out=w2b[:, par * 1024:par * 1024 + width], in_=dram_rows), wse[par])
        return j, par, wse[par].n


    def wstream_mark(j, pe_ms):
        wstream_pe[j] = pe_ms

    def routing_and_gather(tag):
        for k in range(8):
            V(lambda e, k=k: e.reduce_sum(
                mtmp[:, k:k + 1], macc[:, k:k + 25:8], axis=AX.X))
        P.drain("vector")
        V(lambda e: e.tensor_scalar(meank[:], mtmp[:], 1.0 / TT, None, ALU.mult))
        MK = P.drain("vector", sV)
        P.wait("tensor", sV, MK)
        SCMM = None
        for k in range(8):
            SCMM = T(lambda e, k=k: e.matmul(
                pO[6][0:1, 0:NE], meank[:, k:k + 1], wrsb[:, k * NE:(k + 1) * NE],
                start=(k == 0), stop=(k == 7)), sT if k == 7 else None)
        P.wait("vector", sT, SCMM)
        vchain(
            lambda e: e.tensor_tensor(scor[:], pO[6][0:1, 0:NE], brsb[:], ALU.add),
            lambda e: e.max(out=mx8[:], in_=scor[:]),
            lambda e: e.max_index(out=eidx[:], in_max=mx8[:], in_values=scor[:]),
        )
        ESEL = P.drain("vector", sV)
        P.wait("gpsimd", sV, ESEL)
        eregW = nc.alloc_register(ENG.Pool, f"eW_{tag}")
        evW = RuntimeValue(eregW, min_val=0, max_val=7)
        G(lambda e: e.reg_load(eregW, eidx[0:1, 0:1]))
        G(lambda e: e.dma_start(out=dW1s, in_=dW1[evW]), dSW1)
        G(lambda e: e.dma_start(out=dW2s, in_=dW2[evW]), dSW2)
        P.wait("sync", dSW1, dSW1.n)
        for k in range(8):
            S(lambda e, k=k: e.dma_start(
                out=AR1[:, k * 4096:(k + 1) * 4096],
                in_=dW1s[k * 128:(k + 1) * 128, :]), dW1L)
        W1Q = dW1L.n
        routing_and_gather.w2q = dSW2.n
        P.wait("sync", sV, ESEL)
        eregB = nc.alloc_register(ENG.SP, f"eB_{tag}")
        evB = RuntimeValue(eregB, min_val=0, max_val=7)
        S(lambda e: e.reg_load(eregB, eidx[0:1, 0:1]))
        for half in range(2):
            S(lambda e, half=half: e.dma_start(
                out=b1r, in_=db1[evB][half * 2048:(half + 1) * 2048].unsqueeze(0)),
                dS)
            BQ = dS.n
            P.wait("tensor", dS, BQ)
            BC = None
            for f in range(16):
                BC = T(lambda e, f=f, half=half: e.matmul(
                    pO[6][:, half * 16 + f:half * 16 + f + 1],
                    b1r[0:1, f * 128:(f + 1) * 128],
                    ones1[0:1, 0:1], start=True, stop=True),
                    sT if f == 15 else None)
            P.wait("sync", sT, BC)
        S(lambda e: e.dma_start(out=b2r, in_=db2[evB].unsqueeze(0)), dS)
        BQ2 = dS.n
        P.wait("vector", dS, BQ2)
        P.wait("vector", sT, sT.n)
        V(lambda e: e.tensor_copy(b1c[:], pO[6][:, 0:32]))
        BCV = P.drain("vector", sV)
        P.wait("tensor", sV, BCV)
        P.wait("scalar", sV, BCV)
        return W1Q

    def ffn(tag):
        P.wait("tensor", sV, sV.n)
        W1Q = routing_and_gather(tag)
        evac_sA = None
        for c in range(NFT):
            XLN = ln_apply(c)
            if tag == "f1" and c == 0:
                dbg_tap_sb("xln1", xln[:])
                dbg_tap_sb("lnrs", rs_b[:])
                dbg_tap_sb("lnmrs", mrs_b[:])
            P.wait("tensor", sV, XLN)
            if c == 0:
                P.wait("tensor", dW1L, W1Q)
            else:
                P.wait("tensor", sV, evac_sA)
            relu_ms = {}
            for f in range(32):
                pa = pO[f % 2]
                if f >= 2:
                    P.wait("tensor", sA, relu_ms[f - 2])
                MMF = None
                for k in range(8):
                    MMF = T(lambda e, f=f, k=k, pa=pa: e.matmul(
                        pa[:], AR1[:, k * 4096 + f * 128:k * 4096 + (f + 1) * 128],
                        xln[:, k * FT:(k + 1) * FT],
                        start=(k == 0), stop=(k == 7)), sT if k == 7 else None)
                P.wait("scalar", sT, MMF)
                relu_ms[f] = A(lambda e, f=f, pa=pa: e.activation(
                    AR2[:, f * FT:(f + 1) * FT], pa[:], AF.Relu,
                    bias=b1c[:, f:f + 1]), sA)
            P.wait("tensor", sA, relu_ms[31])
            if tag == "f1" and c == 0:
                P.wait("sync", sA, sA.n)
                dbg_tap_sb("h1", AR2[:])
                dbg_tap_sb("w1sb", AR1[:])
                dbg_tap_sb("b1c", b1c[:])
            if c == 0:
                P.wait("gpsimd", dSW2, routing_and_gather.w2q)
            last_b = None
            for fk in range(32):
                j, par, wq = stream_wrow(dW2s[fk * 128:(fk + 1) * 128, :], 1024)
                P.wait("tensor", wse[par], wq)
                MMB = None
                for dm in range(8):
                    MMB = T(lambda e, fk=fk, dm=dm, par=par: e.matmul(
                        PB[dm][:, 0:FT],
                        w2b[:, par * 1024 + dm * 128:par * 1024 + (dm + 1) * 128],
                        AR2[:, fk * FT:(fk + 1) * FT],
                        start=(fk == 0), stop=False),
                        sT if dm == 7 else None)
                wstream_mark(j, MMB)
                last_b = MMB
            LB = None
            for dm in range(8):
                for q in range(4):
                    LB = T(lambda e, dm=dm, q=q: e.matmul(
                        PB[dm][:, q * 128:(q + 1) * 128],
                        b2r[0:1, dm * 128:(dm + 1) * 128],
                        ones1[0:1, :], start=False, stop=(q == 3)),
                        sT if (dm == 7 and q == 3) else None)
            P.wait("vector", sT, LB)
            for dm in range(8):
                V(lambda e, dm=dm, c=c: e.scalar_tensor_tensor(
                    xf[:, dm * TT + c * FT:dm * TT + (c + 1) * FT],
                    PB[dm][:, 0:FT], 0.5,
                    xf[:, dm * TT + c * FT:dm * TT + (c + 1) * FT],
                    ALU.mult, ALU.add), sV)
            evac_sA = sV.n

    def softplus_inplace(t):
        SPX = P.drain("vector", sV)
        P.wait("scalar", sV, SPX)
        A(lambda e, t=t: e.activation(sp1, t[:], AF.Abs))
        P.drain("scalar")
        A(lambda e: e.activation(sp1, sp1, AF.Exp, scale=-1.0))
        P.drain("scalar")
        SPA = A(lambda e: e.activation(sp1, sp1, AF.Ln, bias=1.0), sA)
        P.wait("vector", sA, SPA)
        V(lambda e, t=t: e.tensor_relu(sp2, t[:]))
        P.drain("vector")
        V(lambda e, t=t: e.tensor_tensor(t[:], sp2, sp1, ALU.add))
        P.drain("vector")

    def evac_mtile(m, bank):
        if m < 16:
            A(lambda e, m=m, bank=bank: e.activation(
                AR1[:, ZO + m * FT:ZO + (m + 1) * FT], bank[:], AF.Silu))
            return
        if m < 32 or m == 32 or m == 34:
            if m < 32:
                V(lambda e, bank=bank: e.tensor_copy(xft, bank[:]))
            elif m == 32:
                V(lambda e, bank=bank: e.tensor_copy(brot4[0:32, :], bank[0:32, :]))
                V(lambda e, bank=bank: e.tensor_copy(bfz[32:64, :], bank[32:64, :]))
                V(lambda e, bank=bank: e.tensor_copy(bfz[64:128, :], bank[64:128, :]))
                V(lambda e, bank=bank: e.tensor_copy(xft, bank[:]))
            else:
                V(lambda e, bank=bank: e.tensor_copy(lntmp[:], bank[:]))
            XE = P.drain("vector", sV)
            P.wait("tensor", sV, XE)
            TE = None
            for sub in range(4):
                if m == 34:
                    TE = T(lambda e, sub=sub: e.transpose(
                        pT32[:, sub * 128:(sub + 1) * 128],
                        fstage[:, sub * 128:(sub + 1) * 128], idf[:]),
                        sT if sub == 3 else None)
                else:
                    TE = T(lambda e, sub=sub: e.transpose(
                        pTb[:, sub * 128:(sub + 1) * 128],
                        xft[:, sub * 128:(sub + 1) * 128], idbf[:]),
                        sT if sub == 3 else None)
            P.wait("vector", sT, TE)
            if m < 32:
                mx = m - 16
                for sub in range(4):
                    V(lambda e, sub=sub, mx=mx: e.tensor_copy(
                        AR1[:, XO + sub * DIN + mx * 128:XO + sub * DIN + (mx + 1) * 128],
                        pTb[:, sub * 128:(sub + 1) * 128]))
            elif m == 32:
                for sub in range(4):
                    V(lambda e, sub=sub: e.tensor_copy(
                        btokz[:, sub * 128 + 32:(sub + 1) * 128],
                        pTb[:, sub * 128 + 32:(sub + 1) * 128]))
                    V(lambda e, sub=sub: e.tensor_copy(
                        brt[:, sub * 32:(sub + 1) * 32],
                        pTb[:, sub * 128:sub * 128 + 32]))
            else:
                V(lambda e: e.tensor_copy(smalls, pT32[:, 0:512]))
            TEV = P.drain("vector", sV)
            P.wait("tensor", sV, TEV)
            return
        V(lambda e, bank=bank: e.tensor_copy(cff[:, :], bank[:]))
        V(lambda e, bank=bank: e.tensor_copy(crot4[0:32, :], bank[0:32, :]))
        V(lambda e, bank=bank: e.tensor_copy(cshz[32:64, :], bank[32:64, :]))
        V(lambda e, bank=bank: e.tensor_copy(cshz[64:128, :], bank[64:128, :]))

    def emit_chunk(b, cc):
        par = cc % 2
        P.wait("sync", sV, sV.n)
        S(lambda e, cc=cc, par=par: e.dma_start(
            out=AR1[1:128, SO + par * DIN:SO + (par + 1) * DIN],
            in_=AR1[0:127, XO + cc * DIN:XO + (cc + 1) * DIN]), dR)
        if cc == 0:
            S(lambda e, par=par: e.dma_start(
                out=AR1[0:1, SO + par * DIN:SO + par * DIN + DIN],
                in_=stash[:]), dR)
        else:
            S(lambda e, cc=cc, par=par: e.dma_start(
                out=AR1[0:1, SO + par * DIN:SO + par * DIN + DIN],
                in_=AR1[127:128, XO + (cc - 1) * DIN:XO + cc * DIN]), dR)
        XSH = dR.n

        dthall = fstage
        psiall = smalls
        for gh in range(2):
            P.wait("vector", sA, sA.n)
            for g4 in range(4):
                V(lambda e, g4=g4, gh=gh, cc=cc: e.tensor_tensor(
                    dthall[:, g4 * 128:(g4 + 1) * 128]
                    .rearrange("p (a r) -> p a r", a=4),
                    dtv[:, cc * 32 + (gh * 4 + g4) * 4:
                        cc * 32 + (gh * 4 + g4 + 1) * 4]
                    .unsqueeze(2).to_broadcast([128, 4, 32]),
                    tanga[:, cc * 32:(cc + 1) * 32]
                    .unsqueeze(1).to_broadcast([128, 4, 32]),
                    ALU.mult))
            DTH = P.drain("vector", sV)
            P.wait("tensor", sV, DTH)
            PHI = None
            for g4 in range(4):
                PHI = T(lambda e, g4=g4: e.matmul(
                    pO[6][:, g4 * 128:(g4 + 1) * 128],
                    dthall[:, g4 * 128:(g4 + 1) * 128], Lf[:],
                    start=True, stop=True), sT if g4 == 3 else None)
            P.wait("vector", sT, PHI)
            vchain(
                lambda e: e.tensor_scalar(psiall, pO[6][:, 0:512], 1.0 / (2 * PI),
                                          None, ALU.mult),
                *[(lambda e, q=q: e.tensor_copy(kint, psiall[:, q * 128:(q + 1) * 128]))
                  if half == 0 else
                  (lambda e, q=q: e.tensor_copy(psiall[:, q * 128:(q + 1) * 128], kint))
                  for q in range(4) for half in range(2)],
                lambda e: e.scalar_tensor_tensor(psiall, psiall, -2 * PI,
                                                 pO[6][:, 0:512], ALU.mult,
                                                 ALU.add),
                lambda e: e.tensor_scalar(dthall, psiall, PI, None, ALU.is_gt),
                lambda e: e.scalar_tensor_tensor(psiall, dthall, -2 * PI, psiall,
                                                 ALU.mult, ALU.add),
                lambda e: e.tensor_scalar(dthall, psiall, -PI, None, ALU.is_lt),
                lambda e: e.scalar_tensor_tensor(psiall, dthall, 2 * PI, psiall,
                                                 ALU.mult, ALU.add),
                lambda e: e.tensor_scalar(psiall, psiall, -PI, PI, ALU.max,
                                          ALU.min),
            )
            PSID = P.drain("vector", sV)
            P.wait("scalar", sV, PSID)
            A(lambda e, gh=gh: e.activation(sinf[:, gh * 512:(gh + 1) * 512],
                                            psiall, AF.Sin))
            A(lambda e: e.activation(dthall, psiall, AF.Abs))
            P.drain("scalar")
            PSIA = A(lambda e, gh=gh: e.activation(
                cosf[:, gh * 512:(gh + 1) * 512], dthall, AF.Sin,
                bias=hpi[:, 0:1], scale=-1.0), sA)
            P.wait("tensor", sA, PSIA)
            P.wait("vector", sA, PSIA)
        for half in range(4):
            src = (sinf, cosf)[half % 2]
            dst = (sintk, costk)[half % 2]
            goff = (half // 2) * 4
            TE = None
            for g4 in range(4):
                TE = T(lambda e, src=src, g4=g4, goff=goff: e.transpose(
                    pTb[:, g4 * 128:(g4 + 1) * 128],
                    src[:, (goff + g4) * 128:(goff + g4 + 1) * 128], idbf[:]),
                    sT if g4 == 3 else None)
            P.wait("vector", sT, TE)
            for g4 in range(4):
                V(lambda e, dst=dst, g4=g4, goff=goff: e.tensor_copy(
                    dst[:, (goff + g4) * 128:(goff + g4 + 1) * 128],
                    pTb[:, g4 * 128:(g4 + 1) * 128]))
            TEV = P.drain("vector", sV)
            P.wait("tensor", sV, TEV)
        LMM = T(lambda e, cc=cc: e.matmul(pO[6][0:32, 0:128],
                                          admt[:, cc * 32:(cc + 1) * 32], nLf[:],
                                          start=True, stop=True), sT)
        P.wait("vector", sT, LMM)
        P.wait("vector", sA, sA.n)
        vchain(
            lambda e: e.tensor_copy(lfr, pO[6][0:32, 0:128]),
            lambda e: e.tensor_copy(lhr, lfr),
            lambda e: e.tensor_copy(sp1[0:32, 0:128], lhr),
            lambda e: e.tensor_tensor(sp2[0:32, 0:128], lfr, sp1[0:32, 0:128],
                                      ALU.subtract),
            lambda e: e.tensor_copy(llr, sp2[0:32, 0:128]),
        )
        LRV = P.drain("vector", sV)
        P.wait("tensor", sV, LRV)
        LTP = T(lambda e: e.transpose(pT32[:, 0:32], lfr, idf[0:32, 0:32]), sT)
        P.wait("vector", sT, LTP)
        V(lambda e: e.tensor_scalar(negltok, pT32[:, 0:32], -1.0, None, ALU.mult))
        NLT = P.drain("vector", sV)
        P.wait("tensor", sV, NLT)
        P.wait("scalar", sV, NLT)
        P.wait("vector", dR, XSH)

        WCM = T(lambda e, cc=cc: e.matmul(
            pT32[:, 0:32], Lsn[:], admt[:, cc * 32:(cc + 1) * 32],
            start=True, stop=True), sT)
        P.wait("scalar", sT, WCM)
        WCA = A(lambda e: e.activation(wcall, pT32[:, 0:32], AF.Exp), sA)
        bet3 = betav[:, cc * 32:(cc + 1) * 32].unsqueeze(2).to_broadcast([128, 32, 64])
        gam3 = gammav[:, cc * 32:(cc + 1) * 32].unsqueeze(2).to_broadcast([128, 32, 64])
        wc3 = wcall.unsqueeze(2).to_broadcast([128, 32, 64])
        xsh3 = AR1[:, SO + par * DIN:SO + (par + 1) * DIN].rearrange(
            "p (h d) -> p h d", h=32)
        xtk3 = AR1[:, XO + cc * DIN:XO + (cc + 1) * DIN].rearrange(
            "p (h d) -> p h d", h=32)
        vall3 = vall.rearrange("p (h d) -> p h d", h=32)
        vdall3 = vdall.rearrange("p (h d) -> p h d", h=32)
        V(lambda e: e.tensor_tensor(vall3, xsh3, bet3, ALU.mult))
        V(lambda e: e.tensor_tensor(vdall3, xtk3, gam3, ALU.mult))
        P.drain("vector")
        V(lambda e: e.tensor_tensor(vall3, vall3, vdall3, ALU.add))
        P.wait("vector", sA, WCA)
        P.drain("vector")
        V(lambda e: e.tensor_tensor(vdall3, vall3, wc3, ALU.mult))
        VBAT = P.drain("vector", sV)
        P.wait("tensor", sV, VBAT)
        P.wait("tensor", sA, sA.n)
        dbg_tap_sb("vall", vall)
        dbg_tap_sb("vdall", vdall)
        dbg_tap_sb("wcall", wcall)
        dbg_tap_sb("cosf", cosf)
        dbg_tap_sb("sinf", sinf)
        dbg_tap_sb("negltok", negltok)
        dbg_tap_sb("bfz", bfz)
        dbg_tap_sb("cff", cff)
        dbg_tap_sb("xtok", AR1[:, XO:XO + 8192])

        g_ms, ys_ms = {}, {}
        e_ms, d_ms, spt_ms, post_ms, prep_ms = {}, {}, {}, {}, {}
        BLB = [pO[6][:, 0:128], pT32[:, 0:128]]

        for s in range(NH + 1):
            if s < NH:
                hp = s % 2
                if s >= 2:
                    P.wait("tensor", sA, d_ms[s - 2])
                T(lambda e, s=s, hp=hp: e.matmul(
                    BLB[hp], idbf[0:32, s:s + 1].to_broadcast([32, 128]), lhr,
                    start=True, stop=False))
                BL12 = T(lambda e, s=s, hp=hp: e.matmul(
                    BLB[hp], idbf[0:32, s:s + 1].to_broadcast([32, 128]), llr,
                    start=False, stop=True), sT)
                if s >= 2:
                    P.wait("scalar", sV, post_ms[s - 2])
                P.wait("scalar", sT, BL12)
                e_ms[s] = A(lambda e, s=s, hp=hp: e.activation(
                    ebuf[:, hp * 128:(hp + 1) * 128], BLB[hp], AF.Exp), sA)
            if s >= 1:
                h = s - 1
                hp = h % 2
                P.wait("tensor", sV, prep_ms[h])
                T(lambda e, hp=hp, cc=cc: e.matmul(
                    pO[hp][:, 0:128], bfz[:, cc * 128:(cc + 1) * 128],
                    cff[:, cc * 128:(cc + 1) * 128], start=True, stop=False))
                g_ms[h] = T(lambda e, hp=hp: e.matmul(
                    pO[hp][:, 0:128], krf[:, hp * 128:(hp + 1) * 128],
                    qrf[:, hp * 128:(hp + 1) * 128], start=False, stop=True), sT)
            if s < NH:
                hp = s % 2
                P.wait("tensor", sA, e_ms[s])
                BL3 = T(lambda e, hp=hp: e.matmul(
                    BLB[hp], idbf[:], cneg[:], start=False, stop=True,
                    skip_group_check=True), sT)
                if s >= 2:
                    P.wait("scalar", sV, spt_ms[s - 2])
                P.wait("scalar", sT, BL3)
                d_ms[s] = A(lambda e, s=s, hp=hp: e.activation(
                    dbuf[:, hp * 128:(hp + 1) * 128], BLB[hp], AF.Exp,
                    bias=negltok[:, s:s + 1]), sA)
            if s >= 1:
                h = s - 1
                hp = h % 2
                P.wait("vector", sT, g_ms[h])
                P.wait("vector", sA, d_ms[h])
                V(lambda e, hp=hp: e.tensor_tensor(
                    spt[:, hp * 128:(hp + 1) * 128], pO[hp][:, 0:128],
                    dbuf[:, hp * 128:(hp + 1) * 128], ALU.mult))
                spt_ms[h] = P.drain("vector", sV)
                if h >= 2:
                    P.wait("tensor", sV, post_ms[h - 2])
                P.wait("tensor", sV, spt_ms[h])
                T(lambda e, h=h, hp=hp: e.matmul(
                    pO[2 + hp][0:64, 0:128], vall[:, h * 64:(h + 1) * 64],
                    spt[:, hp * 128:(hp + 1) * 128], start=True, stop=False))
                T(lambda e, h=h: e.matmul(
                    pO[2 + (h % 2)][0:64, 0:128], sshp[:, h * 64:(h + 1) * 64],
                    qhs[:, (h % 2) * 128:((h % 2) + 1) * 128],
                    start=False, stop=False))
                T(lambda e, h=h: e.matmul(
                    pO[2 + (h % 2)][0:64, 0:128], srot_bf[:, h * 64:(h + 1) * 64],
                    qhr[:, (h % 2) * 128:((h % 2) + 1) * 128],
                    start=False, stop=False))
                T(lambda e, hp=hp: e.matmul(
                    pO[2 + hp][0:64, 0:128], xdh[:, hp * 64:(hp + 1) * 64],
                    idbf[:], start=False, stop=True))
                T(lambda e, h=h, hp=hp, cc=cc: e.matmul(
                    pO[4 + hp][:, 0:64], btokz[:, cc * 128:(cc + 1) * 128],
                    vdall[:, h * 64:(h + 1) * 64], start=True, stop=True))
                ys_ms[h] = T(lambda e, h=h, hp=hp: e.matmul(
                    pO[4 + hp][0:64, 64:128], krt[:, hp * 64:(hp + 1) * 64],
                    vdall[:, h * 64:(h + 1) * 64], start=True, stop=True), sT)
            if s < NH:
                hp = s % 2
                hb = (s % 4) * 32
                gg = s // 4
                if s >= 2:
                    P.wait("vector", sT, ys_ms[s - 2])
                V(lambda e, hp=hp, hb=hb, gg=gg, cc=cc: e.tensor_tensor(
                    krf[0:32, hp * 128:(hp + 1) * 128],
                    brot4[hb:hb + 32, cc * 128:(cc + 1) * 128],
                    cosf[hb:hb + 32, gg * 128:(gg + 1) * 128], ALU.mult))
                V(lambda e, hp=hp, hb=hb, gg=gg, cc=cc: e.tensor_tensor(
                    krf[32:64, hp * 128:(hp + 1) * 128],
                    brot4[hb:hb + 32, cc * 128:(cc + 1) * 128],
                    sinf[hb:hb + 32, gg * 128:(gg + 1) * 128], ALU.mult))
                V(lambda e, hp=hp, hb=hb, gg=gg, cc=cc: e.tensor_tensor(
                    qrf[0:32, hp * 128:(hp + 1) * 128],
                    crot4[hb:hb + 32, cc * 128:(cc + 1) * 128],
                    cosf[hb:hb + 32, gg * 128:(gg + 1) * 128], ALU.mult))
                V(lambda e, hp=hp, hb=hb, gg=gg, cc=cc: e.tensor_tensor(
                    qrf[32:64, hp * 128:(hp + 1) * 128],
                    crot4[hb:hb + 32, cc * 128:(cc + 1) * 128],
                    sinf[hb:hb + 32, gg * 128:(gg + 1) * 128], ALU.mult))
                V(lambda e, s=s, hp=hp, cc=cc: e.tensor_tensor(
                    krt[:, hp * 64:hp * 64 + 32], brt[:, cc * 32:(cc + 1) * 32],
                    costk[:, s * 32:(s + 1) * 32], ALU.mult))
                V(lambda e, s=s, hp=hp, cc=cc: e.tensor_tensor(
                    krt[:, hp * 64 + 32:hp * 64 + 64], brt[:, cc * 32:(cc + 1) * 32],
                    sintk[:, s * 32:(s + 1) * 32], ALU.mult))
                V(lambda e, s=s, cc=cc, par=par: e.tensor_scalar(
                    sp2[:, 0:64],
                    AR1[:, SO + par * DIN + s * 64:SO + par * DIN + (s + 1) * 64],
                    betav[:, cc * 32 + s:cc * 32 + s + 1], None, ALU.mult))
                P.drain("vector")
                V(lambda e, s=s, hp=hp, cc=cc: e.scalar_tensor_tensor(
                    vti[:, hp * 64:(hp + 1) * 64],
                    AR1[:, XO + cc * DIN + s * 64:XO + cc * DIN + (s + 1) * 64],
                    gammav[:, cc * 32 + s:cc * 32 + s + 1], sp2[:, 0:64],
                    ALU.mult, ALU.add))
                V(lambda e, s=s, hp=hp, cc=cc: e.tensor_scalar(
                    xdh[:, hp * 64:(hp + 1) * 64],
                    AR1[:, XO + cc * DIN + s * 64:XO + cc * DIN + (s + 1) * 64],
                    Dbc[:, s:s + 1], None, ALU.mult))
                P.wait("vector", sA, e_ms[s])
                V(lambda e, hp=hp: e.tensor_tensor(
                    qhr[:, hp * 128:(hp + 1) * 128],
                    qrf[0:64, hp * 128:(hp + 1) * 128],
                    ebuf[0:64, hp * 128:(hp + 1) * 128], ALU.mult))
                V(lambda e, hp=hp, cc=cc: e.tensor_tensor(
                    qhs[:, hp * 128:(hp + 1) * 128],
                    cshz[:, cc * 128:(cc + 1) * 128],
                    ebuf[:, hp * 128:(hp + 1) * 128], ALU.mult))
                prep_ms[s] = P.drain("vector", sV)
            if s >= 1:
                h = s - 1
                hp = h % 2
                hb = (h % 4) * 32
                gg = h // 4
                P.wait("vector", sT, ys_ms[h])
                V(lambda e, h=h, hp=hp, cc=cc: e.tensor_tensor(
                    AR1[(h % 2) * 64:(h % 2) * 64 + 64,
                        YO + (h // 2) * FT + cc * 128:YO + (h // 2) * FT + (cc + 1) * 128],
                    pO[2 + hp][0:64, 0:128],
                    AR1[(h % 2) * 64:(h % 2) * 64 + 64,
                        ZO + (h // 2) * FT + cc * 128:ZO + (h // 2) * FT + (cc + 1) * 128],
                    ALU.mult))
                V(lambda e, hb=hb, gg=gg: e.tensor_copy(
                    rc0[:, 0:1], cosf[hb:hb + 32, gg * 128 + 127:gg * 128 + 128]))
                V(lambda e, hb=hb, gg=gg: e.tensor_copy(
                    rc0[:, 1:2], sinf[hb:hb + 32, gg * 128 + 127:gg * 128 + 128]))
                P.drain("vector")
                V(lambda e: e.tensor_scalar(rc0[:, 2:3], rc0[:, 1:2], -1.0,
                                            None, ALU.mult))
                V(lambda e, h=h, hp=hp: e.scalar_tensor_tensor(
                    sshp[:, h * 64:(h + 1) * 64], sshp[:, h * 64:(h + 1) * 64],
                    ebuf[:, hp * 128 + 127:hp * 128 + 128],
                    pO[4 + hp][:, 0:64], ALU.mult, ALU.add))
                V(lambda e, h=h, hp=hp: e.scalar_tensor_tensor(
                    ptot, srot_bf[:, h * 64:(h + 1) * 64],
                    ebuf[0:64, hp * 128 + 127:hp * 128 + 128],
                    pO[4 + hp][0:64, 64:128], ALU.mult, ALU.add))
                P.drain("vector")
                V(lambda e: e.tensor_copy(pim0, ptot[32:64, :]))
                P.drain("vector")
                V(lambda e: e.tensor_scalar(sp2[0:32, 0:64], ptot[0:32, :],
                                            rc0[:, 0:1], None, ALU.mult))
                P.drain("vector")
                V(lambda e, h=h: e.scalar_tensor_tensor(
                    srot_bf[0:32, h * 64:(h + 1) * 64], pim0, rc0[:, 1:2],
                    sp2[0:32, 0:64], ALU.mult, ALU.add))
                P.drain("vector")
                V(lambda e: e.tensor_scalar(sp2[0:32, 0:64], pim0,
                                            rc0[:, 0:1], None, ALU.mult))
                P.drain("vector")
                V(lambda e, h=h: e.scalar_tensor_tensor(
                    srot_bf[32:64, h * 64:(h + 1) * 64], ptot[0:32, :],
                    rc0[:, 2:3], sp2[0:32, 0:64], ALU.mult, ALU.add))
                post_ms[h] = P.drain("vector", sV)

    def emit_mamba():
        in_groups = [list(range(6 * i, min(6 * i + 6, 35))) for i in range(6)]
        P.wait("vector", sT, sT.n)
        V(lambda e: e.memset(bfz, 0.0))
        V(lambda e: e.memset(cshz, 0.0))
        V(lambda e: e.memset(btokz, 0.0))
        P.drain("vector", sV)
        for b in range(NB):
            XL2 = ln_apply(b)
            if b == 0:
                dbg_tap_sb("xln2", xln[:])
                dbg_tap_sb("lnrs2", rs_b[:])
            P.wait("tensor", sV, XL2)
            P.wait("gpsimd", sT, sT.n)
            P.wait("vector", dR, dR.n)
            for grp in in_groups:
                gw = len(grp) * 128
                for k in range(8):
                    j, par, wq = stream_wrow(
                        dWinT[k * 128:(k + 1) * 128,
                              grp[0] * 128:grp[0] * 128 + gw], gw)
                    P.wait("tensor", wse[par], wq)
                    MM = None
                    for mi in range(len(grp)):
                        MM = T(lambda e, mi=mi, k=k, par=par: e.matmul(
                            pO[mi][:],
                            w2b[:, par * 1024 + mi * 128:par * 1024 + (mi + 1) * 128],
                            xln[:, k * FT:(k + 1) * FT],
                            start=(k == 0), stop=(k == 7)),
                            sT if mi == len(grp) - 1 else None)
                    wstream_mark(j, MM)
                P.wait("vector", sT, sT.n)
                P.wait("scalar", sT, sT.n)
                for mi, m in enumerate(grp):
                    evac_mtile(m, pO[mi])
                EV = P.drain("vector", sV)
                EA = P.drain("scalar", sA)
                P.wait("tensor", sV, EV)
                P.wait("tensor", sA, EA)
            P.wait("sync", sV, sV.n)
            for t in (brot4, crot4):
                for q in range(1, 4):
                    S(lambda e, t=t, q=q: e.dma_start(
                        out=t[q * 32:(q + 1) * 32, :], in_=t[0:32, :]), dR)
            REP = dR.n
            P.wait("vector", dR, REP)
            sm = smalls.rearrange("p (s d) -> p s d", s=4)
            V(lambda e, sm=sm: e.tensor_tensor(
                dtv.rearrange("p (s d) -> p s d", s=4), sm[:, :, 0:32],
                dtbc[:, 0:32].unsqueeze(1).to_broadcast([128, 4, 32]), ALU.add))
            P.drain("vector")
            softplus_inplace(dtv)
            V(lambda e, sm=sm: e.tensor_copy(
                admt.rearrange("p (s d) -> p s d", s=4), sm[:, :, 32:64]))
            P.drain("vector")
            softplus_inplace(admt)
            vchain(
                lambda e: e.tensor_scalar(admt, admt, 1e-4, None, ALU.max),
                lambda e: e.tensor_tensor(admt, admt, dtv, ALU.mult),
            )
            SMV = P.drain("vector", sV)
            P.wait("scalar", sV, SMV)
            A(lambda e: e.activation(sp1, admt, AF.Exp, scale=-1.0))
            A(lambda e, sm=sm: e.activation(
                sp2.rearrange("p (s d) -> p s d", s=4), sm[:, :, 64:96],
                AF.Sigmoid))
            P.drain("scalar")
            SMA = A(lambda e, sm=sm: e.activation(
                tanga.rearrange("p (s d) -> p s d", s=4), sm[:, :, 96:128],
                AF.Tanh), sA)
            P.wait("vector", sA, SMA)
            vchain(
                lambda e: e.tensor_tensor(gammav, sp2, dtv, ALU.mult),
                lambda e: e.tensor_scalar(sp2, sp2, -1.0, 1.0, ALU.mult,
                                          ALU.add),
                lambda e: e.tensor_tensor(betav, sp2, dtv, ALU.mult),
                lambda e: e.tensor_tensor(betav, betav, sp1, ALU.mult),
                lambda e: e.tensor_scalar(tanga, tanga, PI, None, ALU.mult),
            )
            SMALLD = P.drain("vector", sV)
            P.wait("tensor", sV, SMALLD)
            P.wait("scalar", sV, SMALLD)
            for cc in range(NCPB):
                emit_chunk(b, cc)
            P.wait("tensor", sV, sV.n)
            P.wait("gpsimd", sT, sT.n)
            last_o = None
            for fk in range(16):
                j, par, wq = stream_wrow(dWoutT[fk * 128:(fk + 1) * 128, :], 1024)
                P.wait("tensor", wse[par], wq)
                MMO = None
                for dm in range(8):
                    MMO = T(lambda e, fk=fk, dm=dm, par=par: e.matmul(
                        PB[dm][:, 0:FT],
                        w2b[:, par * 1024 + dm * 128:par * 1024 + (dm + 1) * 128],
                        AR1[:, YO + fk * FT:YO + (fk + 1) * FT],
                        start=(fk == 0), stop=(fk == 15)),
                        sT if dm == 7 else None)
                wstream_mark(j, MMO)
                last_o = MMO
            P.wait("vector", sT, last_o)
            for dm in range(8):
                V(lambda e, dm=dm, b=b: e.tensor_add(
                    xf[:, dm * TT + b * TB:dm * TT + (b + 1) * TB],
                    xf[:, dm * TT + b * TB:dm * TT + (b + 1) * TB],
                    PB[dm][:, 0:FT]), sV)
            OPV = sV.n
            P.wait("tensor", sV, OPV)
            if b == 0:
                dbg_tap_sb("ygat", ygat[:] if False else AR1[:, YO:YO + 8192])
                dbg_tap_sb("sshp", sshp[:])
            if b < NB - 1:
                P.wait("sync", sV, OPV)
                S(lambda e: e.dma_start(
                    out=stash[:],
                    in_=AR1[127:128, XO + 3 * DIN:XO + 4 * DIN]), dR)
                P.wait("vector", dR, dR.n)

    def dbg_tap_x(name):
        if name not in dbg_out:
            return
        P.wait("sync", sV, sV.n)
        for k in range(8):
            S(lambda e, k=k, name=name: e.dma_start(
                out=dbg_out[name][k * 128:(k + 1) * 128, :],
                in_=xf[:, k * TT:(k + 1) * TT]), dO)
        P.wait("sync", dO, dO.n)

    _tapped = set()

    def dbg_tap_sb(nm, ap):
        key = [s_ for s_ in dbg_out if s_.startswith("sb:" + nm + ":")]
        if not key or nm in _tapped:
            return
        _tapped.add(nm)
        P.wait("sync", sV, sV.n)
        P.wait("sync", sA, sA.n)
        S(lambda e, k=key[0], ap=ap: e.dma_start(out=dbg_out[k], in_=ap), dO)
        P.wait("sync", dO, dO.n)

    def dbg_tap_eidx(name):
        if name not in dbg_out:
            return
        P.wait("sync", sV, sV.n)
        S(lambda e, name=name: e.dma_start(out=dbg_out[name], in_=eidx[:]), dO)
        P.wait("sync", dO, dO.n)

    ln_stats_and_bcast()
    for c in range(NFT):
        ln_apply(c, with_mean_acc=True)
    ffn("f1")
    dbg_tap_eidx("eidx1")
    dbg_tap_x("x1")
    ln_stats_and_bcast()
    emit_mamba()
    dbg_tap_x("x2")
    ln_stats_and_bcast()
    for c in range(NFT):
        ln_apply(c, with_mean_acc=True)
    ffn("f3")

    FIN = sV.n
    P.wait("sync", sV, FIN)
    for k in range(8):
        S(lambda e, k=k: e.dma_start(out=dY[k * 128:(k + 1) * 128, :],
                                     in_=xf[:, k * TT:(k + 1) * TT]), dO)
    P.wait("sync", dO, dO.n)

    with ctx:
        with nc.Block() as block:
            @block.sync
            def _(e):
                for fn in P.q["sync"]:
                    fn(e)

            @block.gpsimd
            def _(e):
                for fn in P.q["gpsimd"]:
                    fn(e)

            @block.vector
            def _(e):
                for fn in P.q["vector"]:
                    fn(e)

            @block.scalar
            def _(e):
                for fn in P.q["scalar"]:
                    fn(e)

            @block.tensor
            def _(e):
                for fn in P.q["tensor"]:
                    fn(e)
    return nc


def kernel(**inputs):
    x = np.ascontiguousarray(inputs["L0_attn_u"])
    in_common = {
        "w_inT": np.ascontiguousarray(inputs["L0_attn_w_in"].T),
        "w_outT": np.ascontiguousarray(inputs["L0_attn_w_out"].T),
        "Wr": np.ascontiguousarray(inputs["L0_ffn_Wr"]),
        "br": np.ascontiguousarray(inputs["L0_ffn_br"]).reshape(1, NE),
        "W1": np.ascontiguousarray(inputs["L0_ffn_W1"]),
        "b1": np.ascontiguousarray(inputs["L0_ffn_b1"]),
        "W2": np.ascontiguousarray(inputs["L0_ffn_W2"]),
        "b2": np.ascontiguousarray(inputs["L0_ffn_b2"]),
        "dtb": np.ascontiguousarray(inputs["L0_attn_dt_bias"]).reshape(1, NH),
        "Dv": np.ascontiguousarray(inputs["L0_attn_D"]).reshape(1, NH),
    }
    in_maps = []
    for b in range(8):
        m = dict(in_common)
        m["xT"] = np.ascontiguousarray(x[b].T)
        in_maps.append(m)
    nc = build_nc()
    res = run_bass_kernel_spmd(nc, in_maps, core_ids=list(range(8)))
    out = np.stack([np.asarray(r["yT"]).T for r in res.results], axis=0)
    return out.astype(np.float32)


# version 14 (final)
# speedup vs baseline: 1.1528x; 1.0841x over previous
import numpy as np
import ml_dtypes
from contextlib import ExitStack
import concourse.bass as bass
import concourse.mybir as mybir
from concourse.bass_utils import run_bass_kernel_spmd
from concourse.expressions import RuntimeValue

F32 = mybir.dt.float32
BF16 = mybir.dt.bfloat16
U32 = mybir.dt.uint32
AF = mybir.ActivationFunctionType
ALU = mybir.AluOpType
AX = mybir.AxisListType
ENG = mybir.EngineType

PI = float(np.pi)
MAGIC = float(1.5 * 2 ** 23)

DM, TT, DFF, NE = 1024, 2048, 4096, 8
NH, HD = 32, 64
CH, TB = 128, 512
NB, NCPB = TT // TB, TB // CH
FT = 512
NFT = TT // FT
DIN = 2048
DPROJ = 4480

ENGS = ("sync", "gpsimd", "vector", "scalar", "tensor")


class Ctr:
    def __init__(self, sem, step=1):
        self.sem, self.n, self.step = sem, 0, step


class Prog:
    def __init__(self):
        self.q = {e: [] for e in ENGS}

    def do(self, eng, fn, ctr=None, k=None):
        if ctr is None:
            self.q[eng].append(lambda e, fn=fn: fn(e))
            return None
        inc = ctr.step if k is None else k
        ctr.n += inc
        self.q[eng].append(lambda e, fn=fn, c=ctr, i=inc: fn(e).then_inc(c.sem, i))
        return ctr.n

    def wait(self, eng, ctr, val=None):
        v = ctr.n if val is None else val
        if v <= 0:
            return v
        self.q[eng].append(lambda e, s=ctr, v=v: e.wait_ge(s.sem, v))
        return v

    def drain(self, eng, ctr=None, k=None):
        if ctr is None:
            self.q[eng].append(lambda e: e.drain())
            return None
        inc = ctr.step if k is None else k
        ctr.n += inc
        self.q[eng].append(lambda e, c=ctr, i=inc: e.drain().then_inc(c.sem, i))
        return ctr.n


_DBG_STAGES = []


def build_nc():
    nc = bass.Bass("TRN2", target_bir_lowering=False)
    dbg_out = {}
    for st_ in _DBG_STAGES:
        if st_.startswith("x"):
            dbg_out[st_] = nc.dram_tensor("dbg_" + st_, [DM, TT], F32,
                                          kind="ExternalOutput").ap()
        elif st_.startswith("eidx"):
            dbg_out[st_] = nc.dram_tensor("dbg_" + st_, [1, 8], U32,
                                          kind="ExternalOutput").ap()
        elif st_.startswith("sb:"):
            nm, rows, cols, dt = st_.split(":")[1:]
            dbg_out[st_] = nc.dram_tensor("dbg_" + nm, [int(rows), int(cols)],
                                          {"f32": F32, "bf16": BF16}[dt],
                                          kind="ExternalOutput").ap()

    dX = nc.dram_tensor("xT", [DM, TT], F32, kind="ExternalInput").ap()
    dWinT = nc.dram_tensor("w_inT", [DM, DPROJ], F32, kind="ExternalInput").ap()
    dWoutT = nc.dram_tensor("w_outT", [DIN, DM], F32, kind="ExternalInput").ap()
    dWr = nc.dram_tensor("Wr", [DM, NE], F32, kind="ExternalInput").ap()
    dbr = nc.dram_tensor("br", [1, NE], F32, kind="ExternalInput").ap()
    dW1 = nc.dram_tensor("W1", [NE, DM, DFF], F32, kind="ExternalInput").ap()
    db1 = nc.dram_tensor("b1", [NE, DFF], F32, kind="ExternalInput").ap()
    dW2 = nc.dram_tensor("W2", [NE, DFF, DM], F32, kind="ExternalInput").ap()
    db2 = nc.dram_tensor("b2", [NE, DM], F32, kind="ExternalInput").ap()
    ddtb = nc.dram_tensor("dtb", [1, NH], F32, kind="ExternalInput").ap()
    dD = nc.dram_tensor("Dv", [1, NH], F32, kind="ExternalInput").ap()
    dY = nc.dram_tensor("yT", [DM, TT], F32, kind="ExternalOutput").ap()
    dW1s = nc.dram_tensor("w1stage", [DM, DFF], BF16, kind="Internal").ap()
    dW2s = nc.dram_tensor("w2stage", [DFF, DM], BF16, kind="Internal").ap()

    ctx = ExitStack()
    sbt = lambda n, s, d: ctx.enter_context(nc.sbuf_tensor(n, s, d))
    pst = lambda n, s, d=F32: ctx.enter_context(nc.psum_tensor(n, s, d))
    mksem = lambda n, step=1: Ctr(ctx.enter_context(nc.semaphore(n)), step)

    xf = sbt("xf", [128, 8 * TT], F32)
    AR1 = sbt("AR1", [128, 32768], BF16)
    ZO, XO, YO, SO = 0, 8192, 16384, 24576
    AR2 = sbt("AR2", [128, 16384], BF16)
    WSL = 0
    xln = sbt("xln", [128, 8 * FT], BF16)
    w2b = sbt("w2b", [128, 2 * 1024], BF16)
    mrs_b = sbt("mrs_b", [128, TT], BF16)
    rs_b = sbt("rs_b", [128, TT], BF16)
    AR2f = AR2[:].bitcast(F32)
    cosf = AR2[:, 0:1024]
    sinf = AR2[:, 1024:2048]
    costk = AR2[:, 2048:3072]
    sintk = AR2[:, 3072:4096]
    bfz = AR2[:, 4096:4608]
    cff = AR2[:, 4608:5120]
    cshz = AR2[:, 5120:5632]
    brot4 = AR2[:, 5632:6144]
    crot4 = AR2[:, 6144:6656]
    btokz = AR2[:, 6656:7168]
    brt = AR2[:, 7168:7296]
    xft = AR2[:, 7296:7808]
    ebuf = AR2[:, 7808:8064]
    lhr = AR2[0:32, 8064:8192]
    llr = AR2[0:32, 8192:8320]
    krf = AR2[0:64, 8320:8576]
    qrf = AR2[0:64, 8576:8832]
    qhr = AR2[0:64, 8832:9088]
    qhs = AR2[:, 9088:9344]
    krt = AR2[:, 9344:9472]
    vdc = AR2[:, 9600:9728]
    xdh = AR2[:, 9728:9856]
    spt = AR2[:, 9856:10112]
    smalls = AR2f[:, 5248:5760]
    dtv = AR2f[:, 5760:5888]
    admt = AR2f[:, 5888:6016]
    betav = AR2f[:, 6016:6144]
    gammav = AR2f[:, 6144:6272]
    tanga = AR2f[:, 6272:6400]
    sp1 = AR2f[:, 6400:6528]
    sp2 = AR2f[:, 6528:6656]
    negltok = AR2f[:, 6656:6688]
    lfr = AR2f[0:32, 6688:6816]
    dth = AR2f[:, 6816:6944]
    psi = AR2f[:, 6944:7072]
    dbuf = AR2f[:, 7072:7328]
    ptre = [AR2f[0:32, 7328:7392], AR2f[0:32, 7392:7456]]
    ptim = [AR2f[0:32, 6816:6880], AR2f[0:32, 6880:6944]]
    sp2a = [AR2f[0:32, 6528:6592], AR2f[0:32, 6592:6656]]
    sp2b = [AR2f[0:32, 6944:7008], AR2f[0:32, 7008:7072]]
    sp2c = [AR2f[0:32, 6400:6464], AR2f[0:32, 6464:6528]]
    sp2d = [AR2f[0:32, 7556:7620], AR2f[0:32, 7620:7684]]
    tcos = AR2f[0:32, 7460:7492]
    tsin = AR2f[0:32, 7492:7524]
    tnsin = AR2f[0:32, 7524:7556]
    lntmp = AR2f[:, 7460:7972]
    fstage = AR2f[:, 7460:7972]
    kint = AR2[:, 15944:16200].bitcast(mybir.dt.int32)
    wcall = AR2f[:, 7972:8004]
    vall = AR1[:, 28672:30720]
    vdall = AR1[:, 30720:32768]
    xsq = AR2[:, 0:1024]
    xbf = AR2[:, 1024:2048]
    sshp = sbt("sshp", [128, NH * HD], BF16)
    srot_bf = sbt("srot_bf", [64, NH * HD], BF16)
    stash = sbt("stash", [128, 16], BF16)
    idbf = sbt("idbf", [128, 128], BF16)
    idf = sbt("idf", [128, 128], F32)
    cneg = sbt("cneg", [128, 128], BF16)
    Lf = sbt("Lf", [128, 128], F32)
    nLf = sbt("nLf", [128, 128], F32)
    Lsn = sbt("Lsn", [128, 128], F32)
    ones1 = sbt("ones1", [1, 128], F32)
    onebf = sbt("onebf", [128, 128], BF16)
    hpi = sbt("hpi", [128, 1], F32)
    cst = sbt("cst", [128, 2], F32)
    nc.const_aps.register(F32, 1e-5, cst[:, 0:1])
    nc.const_aps.register(F32, 1.0, cst[:, 1:2])
    dtbc = sbt("dtbc", [128, NH], F32)
    Dbc = sbt("Dbc", [128, NH], F32)
    wrsb = sbt("wrsb", [128, 8 * NE], BF16)
    brsb = sbt("brsb", [1, NE], F32)
    rows_ar = sbt("rows_ar", [1, 2048], F32)
    rowb = rows_ar[0:1, 0:2048]
    b1r = rows_ar[0:1, 0:2048]
    b2r = rows_ar[0:1, 0:1024]
    macc = sbt("macc", [128, 8 * NFT], F32)
    mtmp = sbt("mtmp", [128, 8], F32)
    meank = sbt("meank", [128, NE], BF16)
    scor = sbt("scor", [1, NE], F32)
    mx8 = sbt("mx8", [1, 8], F32)
    eidx = sbt("eidx", [1, 8], U32)
    b1c = sbt("b1c", [128, 32], F32)

    pO = [pst(f"pO{i}", [128, FT]) for i in range(7)]
    pTb = pst("pTb", [128, 1024], BF16)
    pT32 = pTb[:].bitcast(F32)
    PB = [pO[i][:] for i in range(7)] + [pT32]

    sT = mksem("sT")
    sV = mksem("sV")
    sA = mksem("sA")
    dS = mksem("dS", 16)
    dG = mksem("dG", 16)
    dR = mksem("dR", 16)
    wse = [mksem("wse0", 16), mksem("wse1", 16)]
    dSW1 = mksem("dSW1", 16)
    dSW2 = mksem("dSW2", 16)
    dW1L = mksem("dW1L", 16)
    dO = mksem("dO", 16)
    sG = mksem("sG")

    P = Prog()
    T = lambda fn, c=None, k=None: P.do("tensor", fn, c, k)
    V = lambda fn, c=None, k=None: P.do("vector", fn, c, k)
    A = lambda fn, c=None, k=None: P.do("scalar", fn, c, k)
    G = lambda fn, c=None, k=None: P.do("gpsimd", fn, c, k)
    S = lambda fn, c=None, k=None: P.do("sync", fn, c, k)

    for k in range(8):
        S(lambda e, k=k: e.dma_start(out=xf[:, k * TT:(k + 1) * TT],
                                     in_=dX[k * 128:(k + 1) * 128, :]), dS)
    S(lambda e: e.dma_start(out=brsb[:], in_=dbr), dS)
    S(lambda e: e.dma_start(out=dtbc[:], in_=ddtb.to_broadcast([128, NH])), dS)
    S(lambda e: e.dma_start(out=Dbc[:], in_=dD.to_broadcast([128, NH])), dS)
    G(lambda e: e.dma_start(out=wrsb[:].rearrange("p (k n) -> p k n", k=8),
                            in_=dWr.rearrange("(k p) n -> p k n", k=8)), dG)
    LD_X, LD_G0 = dS.n, dG.n

    V(lambda e: e.memset(onebf[:], 1.0))
    V(lambda e: e.memset(idf[:], 1.0))
    V(lambda e: e.memset(Lf[:], 1.0))
    V(lambda e: e.memset(cneg[:], 0.0))
    V(lambda e: e.memset(ones1[:], 1.0))
    V(lambda e: e.memset(hpi[:], PI / 2.0))
    V(lambda e: e.memset(cst[:, 0:1], 1e-5))
    V(lambda e: e.memset(cst[:, 1:2], 1.0))
    V(lambda e: e.memset(sshp[:], 0.0))
    V(lambda e: e.memset(srot_bf[:], 0.0))
    V(lambda e: e.memset(stash[:], 0.0))
    V(lambda e: e.memset(macc[:], 0.0))
    MS_ONES = P.drain("vector", sV)
    P.wait("gpsimd", sV, MS_ONES)
    G(lambda e: e.affine_select(idbf[:], onebf[:], pattern=[[-1, 128]], base=0,
                                channel_multiplier=1, compare_op=ALU.is_equal, fill=0.0))
    G(lambda e: e.affine_select(idf[:], idf[:], pattern=[[-1, 128]], base=0,
                                channel_multiplier=1, compare_op=ALU.is_equal, fill=0.0))
    G(lambda e: e.affine_select(cneg[:], cneg[:], pattern=[[1, 128]], base=0,
                                channel_multiplier=-1, compare_op=ALU.is_ge,
                                fill=-1e30))
    G(lambda e: e.affine_select(Lf[:], Lf[:], pattern=[[1, 128]], base=0,
                                channel_multiplier=-1, compare_op=ALU.is_ge, fill=0.0))
    MS_AFF = P.drain("gpsimd", sG)
    P.wait("vector", sG, MS_AFF)
    V(lambda e: e.tensor_scalar(nLf[:], Lf[:], -1.0, None, ALU.mult))
    V(lambda e: e.tensor_scalar(Lsn[:], Lf[:], -1.0, None, ALU.add))
    MS_CONST = P.drain("vector", sV)

    P.wait("tensor", sV, MS_CONST)
    P.wait("tensor", sG, MS_AFF)
    P.wait("tensor", dS, LD_X)
    P.wait("tensor", dG, LD_G0)
    P.wait("scalar", sV, MS_CONST)
    P.wait("scalar", dS, LD_X)
    P.wait("vector", dS, LD_X)
    P.wait("gpsimd", dS, LD_X)
    P.wait("sync", sV, MS_CONST)

    RW_S1, RW_S2, RW_RS, RW_MRS = 0, FT, 2 * FT, 3 * FT

    def vchain(*fns):
        for i, f in enumerate(fns):
            if i:
                P.drain("vector")
            V(f)

    def ln_stats_and_bcast():
        P.wait("tensor", sV, sV.n)
        P.wait("tensor", sA, sA.n)
        P.wait("scalar", sT, sT.n)
        P.wait("vector", sT, sT.n)
        for c in range(NFT):
            stat_ms = {}
            for k in range(8):
                par = k % 2
                if k >= 2:
                    P.wait("scalar", sT, stat_ms[k - 2])
                    P.wait("vector", sT, stat_ms[k - 2])
                A(lambda e, k=k, c=c, par=par: e.activation(
                    xsq[:, par * FT:(par + 1) * FT],
                    xf[:, k * TT + c * FT:k * TT + (c + 1) * FT], AF.Square), sA)
                V(lambda e, k=k, c=c, par=par: e.tensor_copy(
                    xbf[:, par * FT:(par + 1) * FT],
                    xf[:, k * TT + c * FT:k * TT + (c + 1) * FT]), sV)
                P.wait("tensor", sA, sA.n)
                P.wait("tensor", sV, sV.n)
                T(lambda e, par=par, k=k: e.matmul(
                    pO[6][0:1, :], onebf[:, 0:1], xbf[:, par * FT:(par + 1) * FT],
                    start=(k == 0), stop=(k == 7)))
                stat_ms[k] = T(lambda e, par=par, k=k: e.matmul(
                    pT32[0:1, :], onebf[:, 0:1], xsq[:, par * FT:(par + 1) * FT],
                    start=(k == 0), stop=(k == 7)), sT)
            P.wait("vector", sT, sT.n)
            vchain(
                lambda e: e.tensor_scalar(rowb[0:1, RW_S1:RW_S1 + FT], pO[6][0:1, :],
                                          1.0 / DM, None, ALU.mult),
                lambda e: e.tensor_scalar(rowb[0:1, RW_S2:RW_S2 + FT], pT32[0:1, :],
                                          1.0 / DM, None, ALU.mult),
                lambda e: e.scalar_tensor_tensor(rowb[0:1, RW_RS:RW_RS + FT],
                                                 rowb[0:1, RW_S1:RW_S1 + FT], -1.0,
                                                 rowb[0:1, RW_S1:RW_S1 + FT],
                                                 ALU.mult, ALU.mult),
                lambda e: e.tensor_tensor(rowb[0:1, RW_S2:RW_S2 + FT],
                                          rowb[0:1, RW_S2:RW_S2 + FT],
                                          rowb[0:1, RW_RS:RW_RS + FT], ALU.add),
            )
            SVV = P.drain("vector", sV)
            P.wait("scalar", sV, SVV)
            SQQ = A(lambda e: e.activation(rowb[0:1, RW_RS:RW_RS + FT],
                                           rowb[0:1, RW_S2:RW_S2 + FT],
                                           AF.Sqrt, bias=1e-5), sA)
            P.wait("vector", sA, SQQ)
            vchain(
                lambda e: e.reciprocal(rowb[0:1, RW_RS:RW_RS + FT],
                                       rowb[0:1, RW_RS:RW_RS + FT]),
                lambda e: e.scalar_tensor_tensor(rowb[0:1, RW_MRS:RW_MRS + FT],
                                                 rowb[0:1, RW_S1:RW_S1 + FT], -1.0,
                                                 rowb[0:1, RW_RS:RW_RS + FT],
                                                 ALU.mult, ALU.mult),
            )
            RWD = P.drain("vector", sV)
            P.wait("tensor", sV, RWD)
            T(lambda e: e.matmul(pO[6][:, :], ones1[0:1, :],
                                 rowb[0:1, RW_RS:RW_RS + FT], start=True, stop=True))
            BCT = T(lambda e: e.matmul(pT32[:, :], ones1[0:1, :],
                                       rowb[0:1, RW_MRS:RW_MRS + FT],
                                       start=True, stop=True), sT)
            P.wait("vector", sT, BCT)
            V(lambda e, c=c: e.tensor_copy(rs_b[:, c * FT:(c + 1) * FT], pO[6][:, :]))
            V(lambda e, c=c: e.tensor_copy(mrs_b[:, c * FT:(c + 1) * FT], pT32[:, :]),
              sV)
            P.wait("tensor", sV, sV.n)

    def ln_apply(c, with_mean_acc=False):
        P.wait("vector", sT, sT.n)
        xin3 = xf[:].rearrange("p (k t) -> p k t", k=8)[:, :, c * FT:(c + 1) * FT]
        xln3 = xln[:].rearrange("p (k t) -> p k t", k=8)
        rsb3 = rs_b[:, c * FT:(c + 1) * FT].unsqueeze(1).to_broadcast([128, 8, FT])
        mrs3 = mrs_b[:, c * FT:(c + 1) * FT].unsqueeze(1).to_broadcast([128, 8, FT])
        V(lambda e, c=c: e.tensor_tensor(xln3, xin3, rsb3, ALU.mult))
        P.drain("vector")
        V(lambda e, c=c: e.tensor_tensor(xln3, xln3, mrs3, ALU.add))
        P.drain("vector")
        if with_mean_acc:
            for k in range(8):
                V(lambda e, k=k, c=c: e.reduce_sum(
                    macc[:, (c * 8 + k):(c * 8 + k) + 1],
                    xln[:, k * FT:(k + 1) * FT], axis=AX.X))
        return P.drain("vector", sV)

    wstream_i = [0]
    wstream_pe = {}

    def stream_wrow(dram_rows, width):
        j = wstream_i[0]
        wstream_i[0] += 1
        par = j % 2
        if j - 2 in wstream_pe:
            P.wait("gpsimd", sT, wstream_pe[j - 2])
        G(lambda e, dram_rows=dram_rows, par=par, width=width: e.dma_start(
            out=w2b[:, par * 1024:par * 1024 + width], in_=dram_rows), wse[par])
        return j, par, wse[par].n


    def wstream_mark(j, pe_ms):
        wstream_pe[j] = pe_ms

    def routing_and_gather(tag):
        for k in range(8):
            V(lambda e, k=k: e.reduce_sum(
                mtmp[:, k:k + 1], macc[:, k:k + 25:8], axis=AX.X))
        P.drain("vector")
        V(lambda e: e.tensor_scalar(meank[:], mtmp[:], 1.0 / TT, None, ALU.mult))
        MK = P.drain("vector", sV)
        P.wait("tensor", sV, MK)
        SCMM = None
        for k in range(8):
            SCMM = T(lambda e, k=k: e.matmul(
                pO[6][0:1, 0:NE], meank[:, k:k + 1], wrsb[:, k * NE:(k + 1) * NE],
                start=(k == 0), stop=(k == 7)), sT if k == 7 else None)
        P.wait("vector", sT, SCMM)
        vchain(
            lambda e: e.tensor_tensor(scor[:], pO[6][0:1, 0:NE], brsb[:], ALU.add),
            lambda e: e.max(out=mx8[:], in_=scor[:]),
            lambda e: e.max_index(out=eidx[:], in_max=mx8[:], in_values=scor[:]),
        )
        ESEL = P.drain("vector", sV)
        P.wait("gpsimd", sV, ESEL)
        eregW = nc.alloc_register(ENG.Pool, f"eW_{tag}")
        evW = RuntimeValue(eregW, min_val=0, max_val=7)
        G(lambda e: e.reg_load(eregW, eidx[0:1, 0:1]))
        G(lambda e: e.dma_start(out=dW1s, in_=dW1[evW]), dSW1)
        G(lambda e: e.dma_start(out=dW2s, in_=dW2[evW]), dSW2)
        P.wait("sync", dSW1, dSW1.n)
        for k in range(8):
            S(lambda e, k=k: e.dma_start(
                out=AR1[:, k * 4096:(k + 1) * 4096],
                in_=dW1s[k * 128:(k + 1) * 128, :]), dW1L)
        W1Q = dW1L.n
        routing_and_gather.w2q = dSW2.n
        P.wait("sync", sV, ESEL)
        eregB = nc.alloc_register(ENG.SP, f"eB_{tag}")
        evB = RuntimeValue(eregB, min_val=0, max_val=7)
        S(lambda e: e.reg_load(eregB, eidx[0:1, 0:1]))
        for half in range(2):
            S(lambda e, half=half: e.dma_start(
                out=b1r, in_=db1[evB][half * 2048:(half + 1) * 2048].unsqueeze(0)),
                dS)
            BQ = dS.n
            P.wait("tensor", dS, BQ)
            BC = None
            for f in range(16):
                BC = T(lambda e, f=f, half=half: e.matmul(
                    pO[6][:, half * 16 + f:half * 16 + f + 1],
                    b1r[0:1, f * 128:(f + 1) * 128],
                    ones1[0:1, 0:1], start=True, stop=True),
                    sT if f == 15 else None)
            P.wait("sync", sT, BC)
        S(lambda e: e.dma_start(out=b2r, in_=db2[evB].unsqueeze(0)), dS)
        BQ2 = dS.n
        P.wait("vector", dS, BQ2)
        P.wait("vector", sT, sT.n)
        V(lambda e: e.tensor_copy(b1c[:], pO[6][:, 0:32]))
        BCV = P.drain("vector", sV)
        P.wait("tensor", sV, BCV)
        P.wait("scalar", sV, BCV)
        return W1Q

    def ffn(tag):
        P.wait("tensor", sV, sV.n)
        W1Q = routing_and_gather(tag)
        evac_sA = None
        for c in range(NFT):
            XLN = ln_apply(c)
            if tag == "f1" and c == 0:
                dbg_tap_sb("xln1", xln[:])
                dbg_tap_sb("lnrs", rs_b[:])
                dbg_tap_sb("lnmrs", mrs_b[:])
            P.wait("tensor", sV, XLN)
            if c == 0:
                P.wait("tensor", dW1L, W1Q)
            else:
                P.wait("tensor", sV, evac_sA)
            relu_ms = {}
            for f in range(32):
                pa = pO[f % 2]
                if f >= 2:
                    P.wait("tensor", sA, relu_ms[f - 2])
                MMF = None
                for k in range(8):
                    MMF = T(lambda e, f=f, k=k, pa=pa: e.matmul(
                        pa[:], AR1[:, k * 4096 + f * 128:k * 4096 + (f + 1) * 128],
                        xln[:, k * FT:(k + 1) * FT],
                        start=(k == 0), stop=(k == 7)), sT if k == 7 else None)
                P.wait("scalar", sT, MMF)
                relu_ms[f] = A(lambda e, f=f, pa=pa: e.activation(
                    AR2[:, f * FT:(f + 1) * FT], pa[:], AF.Relu,
                    bias=b1c[:, f:f + 1]), sA)
            P.wait("tensor", sA, relu_ms[31])
            if tag == "f1" and c == 0:
                P.wait("sync", sA, sA.n)
                dbg_tap_sb("h1", AR2[:])
                dbg_tap_sb("w1sb", AR1[:])
                dbg_tap_sb("b1c", b1c[:])
            if c == 0:
                P.wait("gpsimd", dSW2, routing_and_gather.w2q)
            last_b = None
            for fk in range(32):
                j, par, wq = stream_wrow(dW2s[fk * 128:(fk + 1) * 128, :], 1024)
                P.wait("tensor", wse[par], wq)
                MMB = None
                for dm in range(8):
                    MMB = T(lambda e, fk=fk, dm=dm, par=par: e.matmul(
                        PB[dm][:, 0:FT],
                        w2b[:, par * 1024 + dm * 128:par * 1024 + (dm + 1) * 128],
                        AR2[:, fk * FT:(fk + 1) * FT],
                        start=(fk == 0), stop=False),
                        sT if dm == 7 else None)
                wstream_mark(j, MMB)
                last_b = MMB
            LB = None
            for dm in range(8):
                for q in range(4):
                    LB = T(lambda e, dm=dm, q=q: e.matmul(
                        PB[dm][:, q * 128:(q + 1) * 128],
                        b2r[0:1, dm * 128:(dm + 1) * 128],
                        ones1[0:1, :], start=False, stop=(q == 3)),
                        sT if (dm == 7 and q == 3) else None)
            P.wait("vector", sT, LB)
            for dm in range(8):
                V(lambda e, dm=dm, c=c: e.scalar_tensor_tensor(
                    xf[:, dm * TT + c * FT:dm * TT + (c + 1) * FT],
                    PB[dm][:, 0:FT], 0.5,
                    xf[:, dm * TT + c * FT:dm * TT + (c + 1) * FT],
                    ALU.mult, ALU.add), sV)
            evac_sA = sV.n

    def softplus_inplace(t):
        SPX = P.drain("vector", sV)
        P.wait("scalar", sV, SPX)
        A(lambda e, t=t: e.activation(sp1, t[:], AF.Abs))
        P.drain("scalar")
        A(lambda e: e.activation(sp1, sp1, AF.Exp, scale=-1.0))
        P.drain("scalar")
        SPA = A(lambda e: e.activation(sp1, sp1, AF.Ln, bias=1.0), sA)
        P.wait("vector", sA, SPA)
        V(lambda e, t=t: e.tensor_relu(sp2, t[:]))
        P.drain("vector")
        V(lambda e, t=t: e.tensor_tensor(t[:], sp2, sp1, ALU.add))
        P.drain("vector")

    def evac_mtile(m, bank):
        if m < 16:
            A(lambda e, m=m, bank=bank: e.activation(
                AR1[:, ZO + m * FT:ZO + (m + 1) * FT], bank[:], AF.Silu))
            return
        if m < 32 or m == 32 or m == 34:
            if m < 32:
                V(lambda e, bank=bank: e.tensor_copy(xft, bank[:]))
            elif m == 32:
                V(lambda e, bank=bank: e.tensor_copy(brot4[0:32, :], bank[0:32, :]))
                V(lambda e, bank=bank: e.tensor_copy(bfz[32:64, :], bank[32:64, :]))
                V(lambda e, bank=bank: e.tensor_copy(bfz[64:128, :], bank[64:128, :]))
                V(lambda e, bank=bank: e.tensor_copy(xft, bank[:]))
            else:
                V(lambda e, bank=bank: e.tensor_copy(lntmp[:], bank[:]))
            XE = P.drain("vector", sV)
            P.wait("tensor", sV, XE)
            TE = None
            for sub in range(4):
                if m == 34:
                    TE = T(lambda e, sub=sub: e.transpose(
                        pT32[:, sub * 128:(sub + 1) * 128],
                        fstage[:, sub * 128:(sub + 1) * 128], idf[:]),
                        sT if sub == 3 else None)
                else:
                    TE = T(lambda e, sub=sub: e.transpose(
                        pTb[:, sub * 128:(sub + 1) * 128],
                        xft[:, sub * 128:(sub + 1) * 128], idbf[:]),
                        sT if sub == 3 else None)
            P.wait("vector", sT, TE)
            if m < 32:
                mx = m - 16
                for sub in range(4):
                    V(lambda e, sub=sub, mx=mx: e.tensor_copy(
                        AR1[:, XO + sub * DIN + mx * 128:XO + sub * DIN + (mx + 1) * 128],
                        pTb[:, sub * 128:(sub + 1) * 128]))
            elif m == 32:
                for sub in range(4):
                    V(lambda e, sub=sub: e.tensor_copy(
                        btokz[:, sub * 128 + 32:(sub + 1) * 128],
                        pTb[:, sub * 128 + 32:(sub + 1) * 128]))
                    V(lambda e, sub=sub: e.tensor_copy(
                        brt[:, sub * 32:(sub + 1) * 32],
                        pTb[:, sub * 128:sub * 128 + 32]))
            else:
                V(lambda e: e.tensor_copy(smalls, pT32[:, 0:512]))
            TEV = P.drain("vector", sV)
            P.wait("tensor", sV, TEV)
            return
        V(lambda e, bank=bank: e.tensor_copy(cff[:, :], bank[:]))
        V(lambda e, bank=bank: e.tensor_copy(crot4[0:32, :], bank[0:32, :]))
        V(lambda e, bank=bank: e.tensor_copy(cshz[32:64, :], bank[32:64, :]))
        V(lambda e, bank=bank: e.tensor_copy(cshz[64:128, :], bank[64:128, :]))

    def emit_chunk(b, cc):
        par = cc % 2
        P.wait("sync", sV, sV.n)
        S(lambda e, cc=cc, par=par: e.dma_start(
            out=AR1[1:128, SO + par * DIN:SO + (par + 1) * DIN],
            in_=AR1[0:127, XO + cc * DIN:XO + (cc + 1) * DIN]), dR)
        if cc == 0:
            S(lambda e, par=par: e.dma_start(
                out=AR1[0:1, SO + par * DIN:SO + par * DIN + DIN],
                in_=stash[:]), dR)
        else:
            S(lambda e, cc=cc, par=par: e.dma_start(
                out=AR1[0:1, SO + par * DIN:SO + par * DIN + DIN],
                in_=AR1[127:128, XO + (cc - 1) * DIN:XO + cc * DIN]), dR)
        XSH = dR.n

        dthall = fstage
        psiall = smalls
        P.wait("vector", sG, sG.n)
        for gh in range(2):
            P.wait("vector", sA, sA.n)
            for g4 in range(4):
                V(lambda e, g4=g4, gh=gh, cc=cc: e.tensor_tensor(
                    dthall[:, g4 * 128:(g4 + 1) * 128]
                    .rearrange("p (a r) -> p a r", a=4),
                    dtv[:, cc * 32 + (gh * 4 + g4) * 4:
                        cc * 32 + (gh * 4 + g4 + 1) * 4]
                    .unsqueeze(2).to_broadcast([128, 4, 32]),
                    tanga[:, cc * 32:(cc + 1) * 32]
                    .unsqueeze(1).to_broadcast([128, 4, 32]),
                    ALU.mult))
            DTH = P.drain("vector", sV)
            P.wait("tensor", sV, DTH)
            PHI = None
            for g4 in range(4):
                PHI = T(lambda e, g4=g4: e.matmul(
                    pO[6][:, g4 * 128:(g4 + 1) * 128],
                    dthall[:, g4 * 128:(g4 + 1) * 128], Lf[:],
                    start=True, stop=True), sT if g4 == 3 else None)
            P.wait("vector", sT, PHI)
            vchain(
                lambda e: e.tensor_scalar(psiall, pO[6][:, 0:512], 1.0 / (2 * PI),
                                          None, ALU.mult),
                *[(lambda e, q=q: e.tensor_copy(kint, psiall[:, q * 128:(q + 1) * 128]))
                  if half == 0 else
                  (lambda e, q=q: e.tensor_copy(psiall[:, q * 128:(q + 1) * 128], kint))
                  for q in range(4) for half in range(2)],
                lambda e: e.scalar_tensor_tensor(psiall, psiall, -2 * PI,
                                                 pO[6][:, 0:512], ALU.mult,
                                                 ALU.add),
                lambda e: e.tensor_scalar(dthall, psiall, PI, None, ALU.is_gt),
                lambda e: e.scalar_tensor_tensor(psiall, dthall, -2 * PI, psiall,
                                                 ALU.mult, ALU.add),
                lambda e: e.tensor_scalar(dthall, psiall, -PI, None, ALU.is_lt),
                lambda e: e.scalar_tensor_tensor(psiall, dthall, 2 * PI, psiall,
                                                 ALU.mult, ALU.add),
                lambda e: e.tensor_scalar(psiall, psiall, -PI, PI, ALU.max,
                                          ALU.min),
            )
            PSID = P.drain("vector", sV)
            P.wait("scalar", sV, PSID)
            A(lambda e, gh=gh: e.activation(sinf[:, gh * 512:(gh + 1) * 512],
                                            psiall, AF.Sin))
            A(lambda e: e.activation(dthall, psiall, AF.Abs))
            P.drain("scalar")
            PSIA = A(lambda e, gh=gh: e.activation(
                cosf[:, gh * 512:(gh + 1) * 512], dthall, AF.Sin,
                bias=hpi[:, 0:1], scale=-1.0), sA)
            P.wait("tensor", sA, PSIA)
            P.wait("vector", sA, PSIA)
        for hb4 in range(4):
            V(lambda e, hb4=hb4: e.tensor_copy(
                tcos[:, hb4 * 8:(hb4 + 1) * 8],
                cosf[hb4 * 32:(hb4 + 1) * 32, 127:1024:128]))
            V(lambda e, hb4=hb4: e.tensor_copy(
                tsin[:, hb4 * 8:(hb4 + 1) * 8],
                sinf[hb4 * 32:(hb4 + 1) * 32, 127:1024:128]))
        TBL = None
        for hb4 in range(4):
            TBL = V(lambda e, hb4=hb4: e.tensor_scalar(
                tnsin[:, hb4 * 8:(hb4 + 1) * 8],
                sinf[hb4 * 32:(hb4 + 1) * 32, 127:1024:128], -1.0, None,
                ALU.mult), sV if hb4 == 3 else None)
        for half in range(4):
            src = (sinf, cosf)[half % 2]
            dst = (sintk, costk)[half % 2]
            goff = (half // 2) * 4
            TE = None
            for g4 in range(4):
                TE = T(lambda e, src=src, g4=g4, goff=goff: e.transpose(
                    pTb[:, g4 * 128:(g4 + 1) * 128],
                    src[:, (goff + g4) * 128:(goff + g4 + 1) * 128], idbf[:]),
                    sT if g4 == 3 else None)
            P.wait("scalar", sT, TE)
            EVC = None
            for g4 in range(4):
                EVC = A(lambda e, dst=dst, g4=g4, goff=goff: e.activation(
                    dst[:, (goff + g4) * 128:(goff + g4 + 1) * 128],
                    pTb[:, g4 * 128:(g4 + 1) * 128], AF.Copy),
                    sA if g4 == 3 else None)
            P.wait("tensor", sA, EVC)
        LMM = T(lambda e, cc=cc: e.matmul(pO[6][0:32, 0:128],
                                          admt[:, cc * 32:(cc + 1) * 32], nLf[:],
                                          start=True, stop=True), sT)
        P.wait("vector", sT, LMM)
        P.wait("vector", sA, sA.n)
        vchain(
            lambda e: e.tensor_copy(lfr, pO[6][0:32, 0:128]),
            lambda e: e.tensor_copy(lhr, lfr),
            lambda e: e.tensor_copy(sp1[0:32, 0:128], lhr),
            lambda e: e.tensor_tensor(sp2[0:32, 0:128], lfr, sp1[0:32, 0:128],
                                      ALU.subtract),
            lambda e: e.tensor_copy(llr, sp2[0:32, 0:128]),
        )
        LRV = P.drain("vector", sV)
        P.wait("tensor", sV, LRV)
        LTP = T(lambda e: e.transpose(pT32[:, 0:32], lfr, idf[0:32, 0:32]), sT)
        P.wait("vector", sT, LTP)
        V(lambda e: e.tensor_scalar(negltok, pT32[:, 0:32], -1.0, None, ALU.mult))
        NLT = P.drain("vector", sV)
        P.wait("tensor", sV, NLT)
        P.wait("scalar", sV, NLT)
        P.wait("vector", dR, XSH)

        WCM = T(lambda e, cc=cc: e.matmul(
            pT32[:, 0:32], Lsn[:], admt[:, cc * 32:(cc + 1) * 32],
            start=True, stop=True), sT)
        P.wait("scalar", sT, WCM)
        WCA = A(lambda e: e.activation(wcall, pT32[:, 0:32], AF.Exp), sA)
        bet3 = betav[:, cc * 32:(cc + 1) * 32].unsqueeze(2).to_broadcast([128, 32, 64])
        gam3 = gammav[:, cc * 32:(cc + 1) * 32].unsqueeze(2).to_broadcast([128, 32, 64])
        wc3 = wcall.unsqueeze(2).to_broadcast([128, 32, 64])
        xsh3 = AR1[:, SO + par * DIN:SO + (par + 1) * DIN].rearrange(
            "p (h d) -> p h d", h=32)
        xtk3 = AR1[:, XO + cc * DIN:XO + (cc + 1) * DIN].rearrange(
            "p (h d) -> p h d", h=32)
        vall3 = vall.rearrange("p (h d) -> p h d", h=32)
        vdall3 = vdall.rearrange("p (h d) -> p h d", h=32)
        V(lambda e: e.tensor_tensor(vall3, xsh3, bet3, ALU.mult))
        V(lambda e: e.tensor_tensor(vdall3, xtk3, gam3, ALU.mult))
        P.drain("vector")
        V(lambda e: e.tensor_tensor(vall3, vall3, vdall3, ALU.add))
        P.wait("vector", sA, WCA)
        P.drain("vector")
        V(lambda e: e.tensor_tensor(vdall3, vall3, wc3, ALU.mult))
        VBAT = P.drain("vector", sV)
        P.wait("tensor", sV, VBAT)
        P.wait("tensor", sA, sA.n)
        dbg_tap_sb("vall", vall)
        dbg_tap_sb("vdall", vdall)
        dbg_tap_sb("wcall", wcall)
        dbg_tap_sb("cosf", cosf)
        dbg_tap_sb("sinf", sinf)
        dbg_tap_sb("negltok", negltok)
        dbg_tap_sb("bfz", bfz)
        dbg_tap_sb("cff", cff)
        dbg_tap_sb("xtok", AR1[:, XO:XO + 8192])

        g_ms, ys_ms = {}, {}
        e_ms, d_ms, spt_ms = {}, {}, {}
        prep_v_ms, prep_g_ms = {}, {}
        post_v_ms, post_a_ms, post_g_ms = {}, {}, {}
        BLB = [pO[6][:, 0:128], pT32[:, 0:128]]
        tcl = lambda s: (s % 4) * 8 + s // 4
        P.wait("scalar", sV, TBL)
        P.wait("gpsimd", sV, TBL)
        P.wait("vector", sA, sA.n)
        P.wait("tensor", sG, sG.n)
        P.wait("vector", sG, sG.n)

        for s in range(NH + 1):
            if s < NH:
                hp = s % 2
                if s >= 2:
                    P.wait("tensor", sA, d_ms[s - 2])
                T(lambda e, s=s, hp=hp: e.matmul(
                    BLB[hp], idbf[0:32, s:s + 1].to_broadcast([32, 128]), lhr,
                    start=True, stop=False))
                BL12 = T(lambda e, s=s, hp=hp: e.matmul(
                    BLB[hp], idbf[0:32, s:s + 1].to_broadcast([32, 128]), llr,
                    start=False, stop=True), sT)
                if s >= 2:
                    P.wait("scalar", sV, post_v_ms[s - 2])
                    P.wait("scalar", sG, prep_g_ms[s - 2])
                P.wait("scalar", sT, BL12)
                e_ms[s] = A(lambda e, s=s, hp=hp: e.activation(
                    ebuf[:, hp * 128:(hp + 1) * 128], BLB[hp], AF.Exp), sA)
            if s >= 1:
                h = s - 1
                hp = h % 2
                P.wait("tensor", sV, prep_v_ms[h])
                T(lambda e, hp=hp, cc=cc: e.matmul(
                    pO[hp][:, 0:128], bfz[:, cc * 128:(cc + 1) * 128],
                    cff[:, cc * 128:(cc + 1) * 128], start=True, stop=False))
                g_ms[h] = T(lambda e, hp=hp: e.matmul(
                    pO[hp][:, 0:128], krf[:, hp * 128:(hp + 1) * 128],
                    qrf[:, hp * 128:(hp + 1) * 128], start=False, stop=True), sT)
            if s < NH:
                hp = s % 2
                P.wait("tensor", sA, e_ms[s])
                BL3 = T(lambda e, hp=hp: e.matmul(
                    BLB[hp], idbf[:], cneg[:], start=False, stop=True,
                    skip_group_check=True), sT)
                if s >= 2:
                    P.wait("scalar", sV, spt_ms[s - 2])
                P.wait("scalar", sT, BL3)
                d_ms[s] = A(lambda e, s=s, hp=hp: e.activation(
                    dbuf[:, hp * 128:(hp + 1) * 128], BLB[hp], AF.Exp,
                    bias=negltok[:, s:s + 1]), sA)
            if s >= 1:
                h = s - 1
                hp = h % 2
                P.wait("vector", sT, g_ms[h])
                P.wait("vector", sA, d_ms[h])
                spt_ms[h] = V(lambda e, hp=hp: e.tensor_tensor(
                    spt[:, hp * 128:(hp + 1) * 128], pO[hp][:, 0:128],
                    dbuf[:, hp * 128:(hp + 1) * 128], ALU.mult), sV)
                if h >= 2:
                    P.wait("tensor", sV, post_v_ms[h - 2])
                P.wait("tensor", sV, spt_ms[h])
                P.wait("tensor", sG, prep_g_ms[h])
                T(lambda e, h=h, hp=hp: e.matmul(
                    pO[2 + hp][0:64, 0:128], vall[:, h * 64:(h + 1) * 64],
                    spt[:, hp * 128:(hp + 1) * 128], start=True, stop=False))
                T(lambda e, h=h: e.matmul(
                    pO[2 + (h % 2)][0:64, 0:128], sshp[:, h * 64:(h + 1) * 64],
                    qhs[:, (h % 2) * 128:((h % 2) + 1) * 128],
                    start=False, stop=False))
                T(lambda e, h=h: e.matmul(
                    pO[2 + (h % 2)][0:64, 0:128], srot_bf[:, h * 64:(h + 1) * 64],
                    qhr[:, (h % 2) * 128:((h % 2) + 1) * 128],
                    start=False, stop=False))
                T(lambda e, hp=hp: e.matmul(
                    pO[2 + hp][0:64, 0:128], xdh[:, hp * 64:(hp + 1) * 64],
                    idbf[:], start=False, stop=True))
                T(lambda e, h=h, hp=hp, cc=cc: e.matmul(
                    pO[4 + hp][:, 0:64], btokz[:, cc * 128:(cc + 1) * 128],
                    vdall[:, h * 64:(h + 1) * 64], start=True, stop=True))
                ys_ms[h] = T(lambda e, h=h, hp=hp: e.matmul(
                    pO[4 + hp][0:64, 64:128], krt[:, hp * 64:(hp + 1) * 64],
                    vdall[:, h * 64:(h + 1) * 64], start=True, stop=True), sT)
            if s < NH:
                hp = s % 2
                hb = (s % 4) * 32
                gg = s // 4
                if s >= 2:
                    P.wait("vector", sT, ys_ms[s - 2])
                    P.wait("vector", sG, prep_g_ms[s - 2])
                V(lambda e, hp=hp, hb=hb, gg=gg, cc=cc: e.tensor_tensor(
                    krf[0:32, hp * 128:(hp + 1) * 128],
                    brot4[hb:hb + 32, cc * 128:(cc + 1) * 128],
                    cosf[hb:hb + 32, gg * 128:(gg + 1) * 128], ALU.mult))
                V(lambda e, hp=hp, hb=hb, gg=gg, cc=cc: e.tensor_tensor(
                    krf[32:64, hp * 128:(hp + 1) * 128],
                    brot4[hb:hb + 32, cc * 128:(cc + 1) * 128],
                    sinf[hb:hb + 32, gg * 128:(gg + 1) * 128], ALU.mult))
                V(lambda e, hp=hp, hb=hb, gg=gg, cc=cc: e.tensor_tensor(
                    qrf[0:32, hp * 128:(hp + 1) * 128],
                    crot4[hb:hb + 32, cc * 128:(cc + 1) * 128],
                    cosf[hb:hb + 32, gg * 128:(gg + 1) * 128], ALU.mult))
                V(lambda e, hp=hp, hb=hb, gg=gg, cc=cc: e.tensor_tensor(
                    qrf[32:64, hp * 128:(hp + 1) * 128],
                    crot4[hb:hb + 32, cc * 128:(cc + 1) * 128],
                    sinf[hb:hb + 32, gg * 128:(gg + 1) * 128], ALU.mult))
                V(lambda e, s=s, hp=hp, cc=cc: e.tensor_tensor(
                    krt[:, hp * 64:hp * 64 + 32], brt[:, cc * 32:(cc + 1) * 32],
                    costk[:, s * 32:(s + 1) * 32], ALU.mult))
                V(lambda e, s=s, hp=hp, cc=cc: e.tensor_tensor(
                    krt[:, hp * 64 + 32:hp * 64 + 64], brt[:, cc * 32:(cc + 1) * 32],
                    sintk[:, s * 32:(s + 1) * 32], ALU.mult))
                prep_v_ms[s] = V(lambda e, s=s, hp=hp, cc=cc: e.tensor_scalar(
                    xdh[:, hp * 64:(hp + 1) * 64],
                    AR1[:, XO + cc * DIN + s * 64:XO + cc * DIN + (s + 1) * 64],
                    Dbc[:, s:s + 1], None, ALU.mult), sV)
            if s < NH:
                hp = s % 2
                P.wait("gpsimd", sV, prep_v_ms[s])
                P.wait("gpsimd", sA, e_ms[s])
                if s >= 2:
                    P.wait("gpsimd", sT, ys_ms[s - 2])
                G(lambda e, hp=hp: e.tensor_tensor(
                    qhr[:, hp * 128:(hp + 1) * 128],
                    qrf[0:64, hp * 128:(hp + 1) * 128],
                    ebuf[0:64, hp * 128:(hp + 1) * 128], ALU.mult))
                prep_g_ms[s] = G(lambda e, hp=hp, cc=cc: e.tensor_tensor(
                    qhs[:, hp * 128:(hp + 1) * 128],
                    cshz[:, cc * 128:(cc + 1) * 128],
                    ebuf[:, hp * 128:(hp + 1) * 128], ALU.mult), sG)
            if s >= 1:
                h = s - 1
                hp = h % 2
                P.wait("vector", sT, ys_ms[h])
                if h >= 2:
                    P.wait("vector", sG, post_g_ms[h - 2])
                V(lambda e, h=h, hp=hp, cc=cc: e.tensor_tensor(
                    AR1[(h % 2) * 64:(h % 2) * 64 + 64,
                        YO + (h // 2) * FT + cc * 128:YO + (h // 2) * FT + (cc + 1) * 128],
                    pO[2 + hp][0:64, 0:128],
                    AR1[(h % 2) * 64:(h % 2) * 64 + 64,
                        ZO + (h // 2) * FT + cc * 128:ZO + (h // 2) * FT + (cc + 1) * 128],
                    ALU.mult))
                V(lambda e, h=h, hp=hp: e.scalar_tensor_tensor(
                    ptre[hp], srot_bf[0:32, h * 64:(h + 1) * 64],
                    ebuf[0:32, hp * 128 + 127:hp * 128 + 128],
                    pO[4 + hp][0:32, 64:128], ALU.mult, ALU.add))
                V(lambda e, h=h, hp=hp: e.scalar_tensor_tensor(
                    ptim[hp], srot_bf[32:64, h * 64:(h + 1) * 64],
                    ebuf[32:64, hp * 128 + 127:hp * 128 + 128],
                    pO[4 + hp][32:64, 64:128], ALU.mult, ALU.add))
                post_v_ms[h] = V(lambda e, h=h, hp=hp: e.scalar_tensor_tensor(
                    sshp[:, h * 64:(h + 1) * 64], sshp[:, h * 64:(h + 1) * 64],
                    ebuf[:, hp * 128 + 127:hp * 128 + 128],
                    pO[4 + hp][:, 0:64], ALU.mult, ALU.add), sV)
            if s >= 1:
                h = s - 1
                hp = h % 2
                P.wait("scalar", sV, post_v_ms[h])
                if h >= 2:
                    P.wait("scalar", sG, post_g_ms[h - 2])
                A(lambda e, h=h, hp=hp: e.activation(
                    sp2a[hp], ptre[hp], AF.Copy,
                    scale=tcos[:, tcl(h):tcl(h) + 1]))
                A(lambda e, h=h, hp=hp: e.activation(
                    sp2b[hp], ptim[hp], AF.Copy,
                    scale=tcos[:, tcl(h):tcl(h) + 1]))
                A(lambda e, h=h, hp=hp: e.activation(
                    sp2c[hp], ptim[hp], AF.Copy,
                    scale=tsin[:, tcl(h):tcl(h) + 1]))
                post_a_ms[h] = A(lambda e, h=h, hp=hp: e.activation(
                    sp2d[hp], ptre[hp], AF.Copy,
                    scale=tnsin[:, tcl(h):tcl(h) + 1]), sA)
            if s >= 1:
                h = s - 1
                hp = h % 2
                P.wait("gpsimd", sA, post_a_ms[h])
                G(lambda e, h=h, hp=hp: e.tensor_tensor(
                    srot_bf[0:32, h * 64:(h + 1) * 64], sp2c[hp], sp2a[hp],
                    ALU.add))
                post_g_ms[h] = G(lambda e, h=h, hp=hp: e.tensor_tensor(
                    srot_bf[32:64, h * 64:(h + 1) * 64], sp2b[hp], sp2d[hp],
                    ALU.add), sG)

    def emit_mamba():
        in_groups = [list(range(6 * i, min(6 * i + 6, 35))) for i in range(6)]
        P.wait("vector", sT, sT.n)
        V(lambda e: e.memset(bfz, 0.0))
        V(lambda e: e.memset(cshz, 0.0))
        V(lambda e: e.memset(btokz, 0.0))
        P.drain("vector", sV)
        for b in range(NB):
            XL2 = ln_apply(b)
            if b == 0:
                dbg_tap_sb("xln2", xln[:])
                dbg_tap_sb("lnrs2", rs_b[:])
            P.wait("tensor", sV, XL2)
            P.wait("gpsimd", sT, sT.n)
            P.wait("vector", dR, dR.n)
            for grp in in_groups:
                gw = len(grp) * 128
                for k in range(8):
                    j, par, wq = stream_wrow(
                        dWinT[k * 128:(k + 1) * 128,
                              grp[0] * 128:grp[0] * 128 + gw], gw)
                    P.wait("tensor", wse[par], wq)
                    MM = None
                    for mi in range(len(grp)):
                        MM = T(lambda e, mi=mi, k=k, par=par: e.matmul(
                            pO[mi][:],
                            w2b[:, par * 1024 + mi * 128:par * 1024 + (mi + 1) * 128],
                            xln[:, k * FT:(k + 1) * FT],
                            start=(k == 0), stop=(k == 7)),
                            sT if mi == len(grp) - 1 else None)
                    wstream_mark(j, MM)
                P.wait("vector", sT, sT.n)
                P.wait("scalar", sT, sT.n)
                for mi, m in enumerate(grp):
                    evac_mtile(m, pO[mi])
                EV = P.drain("vector", sV)
                EA = P.drain("scalar", sA)
                P.wait("tensor", sV, EV)
                P.wait("tensor", sA, EA)
            P.wait("sync", sV, sV.n)
            for t in (brot4, crot4):
                for q in range(1, 4):
                    S(lambda e, t=t, q=q: e.dma_start(
                        out=t[q * 32:(q + 1) * 32, :], in_=t[0:32, :]), dR)
            REP = dR.n
            P.wait("vector", dR, REP)
            sm = smalls.rearrange("p (s d) -> p s d", s=4)
            V(lambda e, sm=sm: e.tensor_tensor(
                dtv.rearrange("p (s d) -> p s d", s=4), sm[:, :, 0:32],
                dtbc[:, 0:32].unsqueeze(1).to_broadcast([128, 4, 32]), ALU.add))
            P.drain("vector")
            softplus_inplace(dtv)
            V(lambda e, sm=sm: e.tensor_copy(
                admt.rearrange("p (s d) -> p s d", s=4), sm[:, :, 32:64]))
            P.drain("vector")
            softplus_inplace(admt)
            vchain(
                lambda e: e.tensor_scalar(admt, admt, 1e-4, None, ALU.max),
                lambda e: e.tensor_tensor(admt, admt, dtv, ALU.mult),
            )
            SMV = P.drain("vector", sV)
            P.wait("scalar", sV, SMV)
            A(lambda e: e.activation(sp1, admt, AF.Exp, scale=-1.0))
            A(lambda e, sm=sm: e.activation(
                sp2.rearrange("p (s d) -> p s d", s=4), sm[:, :, 64:96],
                AF.Sigmoid))
            P.drain("scalar")
            SMA = A(lambda e, sm=sm: e.activation(
                tanga.rearrange("p (s d) -> p s d", s=4), sm[:, :, 96:128],
                AF.Tanh), sA)
            P.wait("vector", sA, SMA)
            vchain(
                lambda e: e.tensor_tensor(gammav, sp2, dtv, ALU.mult),
                lambda e: e.tensor_scalar(sp2, sp2, -1.0, 1.0, ALU.mult,
                                          ALU.add),
                lambda e: e.tensor_tensor(betav, sp2, dtv, ALU.mult),
                lambda e: e.tensor_tensor(betav, betav, sp1, ALU.mult),
                lambda e: e.tensor_scalar(tanga, tanga, PI, None, ALU.mult),
            )
            SMALLD = P.drain("vector", sV)
            P.wait("tensor", sV, SMALLD)
            P.wait("scalar", sV, SMALLD)
            for cc in range(NCPB):
                emit_chunk(b, cc)
            P.wait("tensor", sV, sV.n)
            P.wait("gpsimd", sT, sT.n)
            last_o = None
            for fk in range(16):
                j, par, wq = stream_wrow(dWoutT[fk * 128:(fk + 1) * 128, :], 1024)
                P.wait("tensor", wse[par], wq)
                MMO = None
                for dm in range(8):
                    MMO = T(lambda e, fk=fk, dm=dm, par=par: e.matmul(
                        PB[dm][:, 0:FT],
                        w2b[:, par * 1024 + dm * 128:par * 1024 + (dm + 1) * 128],
                        AR1[:, YO + fk * FT:YO + (fk + 1) * FT],
                        start=(fk == 0), stop=(fk == 15)),
                        sT if dm == 7 else None)
                wstream_mark(j, MMO)
                last_o = MMO
            P.wait("vector", sT, last_o)
            for dm in range(8):
                V(lambda e, dm=dm, b=b: e.tensor_add(
                    xf[:, dm * TT + b * TB:dm * TT + (b + 1) * TB],
                    xf[:, dm * TT + b * TB:dm * TT + (b + 1) * TB],
                    PB[dm][:, 0:FT]), sV)
            OPV = sV.n
            P.wait("tensor", sV, OPV)
            if b == 0:
                dbg_tap_sb("ygat", ygat[:] if False else AR1[:, YO:YO + 8192])
                dbg_tap_sb("sshp", sshp[:])
            if b < NB - 1:
                P.wait("sync", sV, OPV)
                S(lambda e: e.dma_start(
                    out=stash[:],
                    in_=AR1[127:128, XO + 3 * DIN:XO + 4 * DIN]), dR)
                P.wait("vector", dR, dR.n)

    def dbg_tap_x(name):
        if name not in dbg_out:
            return
        P.wait("sync", sV, sV.n)
        for k in range(8):
            S(lambda e, k=k, name=name: e.dma_start(
                out=dbg_out[name][k * 128:(k + 1) * 128, :],
                in_=xf[:, k * TT:(k + 1) * TT]), dO)
        P.wait("sync", dO, dO.n)

    _tapped = set()

    def dbg_tap_sb(nm, ap):
        key = [s_ for s_ in dbg_out if s_.startswith("sb:" + nm + ":")]
        if not key or nm in _tapped:
            return
        _tapped.add(nm)
        P.wait("sync", sV, sV.n)
        P.wait("sync", sA, sA.n)
        S(lambda e, k=key[0], ap=ap: e.dma_start(out=dbg_out[k], in_=ap), dO)
        P.wait("sync", dO, dO.n)

    def dbg_tap_eidx(name):
        if name not in dbg_out:
            return
        P.wait("sync", sV, sV.n)
        S(lambda e, name=name: e.dma_start(out=dbg_out[name], in_=eidx[:]), dO)
        P.wait("sync", dO, dO.n)

    ln_stats_and_bcast()
    for c in range(NFT):
        ln_apply(c, with_mean_acc=True)
    ffn("f1")
    dbg_tap_eidx("eidx1")
    dbg_tap_x("x1")
    ln_stats_and_bcast()
    emit_mamba()
    dbg_tap_x("x2")
    ln_stats_and_bcast()
    for c in range(NFT):
        ln_apply(c, with_mean_acc=True)
    ffn("f3")

    FIN = sV.n
    P.wait("sync", sV, FIN)
    for k in range(8):
        S(lambda e, k=k: e.dma_start(out=dY[k * 128:(k + 1) * 128, :],
                                     in_=xf[:, k * TT:(k + 1) * TT]), dO)
    P.wait("sync", dO, dO.n)

    with ctx:
        with nc.Block() as block:
            @block.sync
            def _(e):
                for fn in P.q["sync"]:
                    fn(e)

            @block.gpsimd
            def _(e):
                for fn in P.q["gpsimd"]:
                    fn(e)

            @block.vector
            def _(e):
                for fn in P.q["vector"]:
                    fn(e)

            @block.scalar
            def _(e):
                for fn in P.q["scalar"]:
                    fn(e)

            @block.tensor
            def _(e):
                for fn in P.q["tensor"]:
                    fn(e)
    return nc


def kernel(**inputs):
    x = np.ascontiguousarray(inputs["L0_attn_u"])
    in_common = {
        "w_inT": np.ascontiguousarray(inputs["L0_attn_w_in"].T),
        "w_outT": np.ascontiguousarray(inputs["L0_attn_w_out"].T),
        "Wr": np.ascontiguousarray(inputs["L0_ffn_Wr"]),
        "br": np.ascontiguousarray(inputs["L0_ffn_br"]).reshape(1, NE),
        "W1": np.ascontiguousarray(inputs["L0_ffn_W1"]),
        "b1": np.ascontiguousarray(inputs["L0_ffn_b1"]),
        "W2": np.ascontiguousarray(inputs["L0_ffn_W2"]),
        "b2": np.ascontiguousarray(inputs["L0_ffn_b2"]),
        "dtb": np.ascontiguousarray(inputs["L0_attn_dt_bias"]).reshape(1, NH),
        "Dv": np.ascontiguousarray(inputs["L0_attn_D"]).reshape(1, NH),
    }
    in_maps = []
    for b in range(8):
        m = dict(in_common)
        m["xT"] = np.ascontiguousarray(x[b].T)
        in_maps.append(m)
    nc = build_nc()
    res = run_bass_kernel_spmd(nc, in_maps, core_ids=list(range(8)))
    out = np.stack([np.asarray(r["yT"]).T for r in res.results], axis=0)
    return out.astype(np.float32)
